# version 19
# baseline (speedup 1.0000x reference)
import math
from contextlib import ExitStack

import numpy as np
import ml_dtypes

import concourse.bass as bass
import concourse.mybir as mybir
from concourse.bass_utils import run_bass_kernel_spmd

F32 = mybir.dt.float32
BF16 = mybir.dt.bfloat16
AF = mybir.ActivationFunctionType
ALU = mybir.AluOpType
AXX = mybir.AxisListType.X

NC = 8
P = 128
S = 8192
D = 4096
NT = 8
NKVH = 8
GRP = 4
NQH = 32
DC = 32
EXP = 64
BOT = 512
HC = 8
EPS = 1e-5
SCL = 1.0 / math.sqrt(128.0)

SLOT_OF = [8 * (g % 8) + g // 8 for g in range(64)]


def build_nc(taps=False):
    nc = bass.Bass("TRN2", target_bir_lowering=False, debug=False)

    x_in = nc.dram_tensor("x_in", [NT * P, D], F32, kind="ExternalInput").ap()
    wq = nc.dram_tensor("wq", [NQH, P, DC, P], F32, kind="ExternalInput").ap()
    wk = nc.dram_tensor("wk", [NKVH, P, DC, P], F32, kind="ExternalInput").ap()
    wv = nc.dram_tensor("wv", [NKVH, P, DC, P], F32, kind="ExternalInput").ap()
    wo = nc.dram_tensor("wo", [HC, P, DC, 512], F32, kind="ExternalInput").ap()
    aw = nc.dram_tensor("aw", [4, P, DC, P], F32, kind="ExternalInput").ap()
    bw = nc.dram_tensor("bw", [HC, P, 4, 512], F32, kind="ExternalInput").ap()
    wqw = nc.dram_tensor("wqw", [P, DC, 16], F32, kind="ExternalInput").ap()
    keyst = nc.dram_tensor("keyst", [16, EXP], F32, kind="ExternalInput").ap()
    e2 = nc.dram_tensor("e2", [EXP, BOT], F32, kind="ExternalInput").ap()
    tabs_in = nc.dram_tensor("tabs_in", [4, P, NT * P], F32,
                             kind="ExternalInput").ap()
    masks_in = nc.dram_tensor("masks", [P, 8, GRP * P], BF16,
                              kind="ExternalInput").ap()
    y_out = nc.dram_tensor("y_out", [NT * P, D], F32, kind="ExternalOutput").ap()

    tap = {}
    if taps:
        tap["xnT"] = nc.dram_tensor("t_xnT", [P, DC * NT * P // 128, 128], BF16,
                                    kind="ExternalOutput").ap()
        tap["qa1"] = nc.dram_tensor("t_qa1", [P, NKVH * NT * GRP * P], BF16,
                                    kind="ExternalOutput").ap()
        tap["cc"] = nc.dram_tensor("t_cc", [256, 65536], BF16,
                                   kind="ExternalOutput").ap()
        tap["gex"] = nc.dram_tensor("t_gex", [P, 4 * NT * P], BF16,
                                    kind="ExternalOutput").ap()
        tap["qr"] = nc.dram_tensor("t_qr", [16, NT * P], BF16,
                                   kind="ExternalOutput").ap()
        tap["zg"] = nc.dram_tensor("t_zg", [P, 4 * NT * P], BF16,
                                   kind="ExternalOutput").ap()
        tap["qa2"] = nc.dram_tensor("t_qa2", [P, NKVH * NT * GRP * P], BF16,
                                    kind="ExternalOutput").ap()

    cc_in = nc.dram_tensor("cc_in", [32, 65536], BF16)
    cc_out = nc.dram_tensor("cc_out", [256, 65536], BF16, addr_space="Shared")

    plan = []

    def em(engine, fn):
        plan.append((engine, fn))

    n = dict(pe=0, dve=0, act=0, pool=0, cc=0,
             dw=0, dx=0, dtab=0, dcont=0, dkv=0, dx2=0, dout=0, dtap=0)

    def bump(key, amt=1):
        n[key] += amt
        return n[key]

    with ExitStack() as top:
        E = top.enter_context
        sem = {k: E(nc.semaphore(k)) for k in n}

        QA = E(nc.sbuf_tensor("QA", [P, NKVH, NT, GRP, P], BF16))
        zgT = E(nc.sbuf_tensor("zgT", [P, 4, NT * P], BF16))
        ident = E(nc.sbuf_tensor("ident", [P, P], BF16))
        ones_bf = E(nc.sbuf_tensor("ones_bf", [P, P], BF16))
        keyst_bf = E(nc.sbuf_tensor("keyst_bf", [16, EXP], BF16))
        e2_bf = E(nc.sbuf_tensor("e2_bf", [EXP, BOT], BF16))

        def f(g):
            g.memset(ident[:, :], 0.0)
            g.affine_select(out=ident[:, :], in_=ident[:, :],
                            compare_op=ALU.not_equal, fill=1.0, base=0,
                            pattern=[[-1, P]], channel_multiplier=1)
            g.memset(ones_bf[:, :], 1.0).then_inc(sem["pool"], 1)
        em("gpsimd", f)
        POOL_CONST = bump("pool")

        with ExitStack() as scA:
            EA = scA.enter_context
            xnT = EA(nc.sbuf_tensor("xnT", [P, DC, NT * P], BF16))
            tab = EA(nc.sbuf_tensor("tab", [P, 2, NT * P], F32))
            wbf = [EA(nc.sbuf_tensor(f"wbf{i}", [P, D], BF16)) for i in range(2)]
            ropet = EA(nc.sbuf_tensor("ropet", [P, 512], F32))
            ropec = EA(nc.sbuf_tensor("ropec", [P, 512], F32))
            kstg = [EA(nc.sbuf_tensor(f"kstg{i}", [P, 512], BF16))
                    for i in range(2)]
            vstg = [EA(nc.sbuf_tensor(f"vstg{i}", [P, 512], BF16))
                    for i in range(2)]
            scLN = ExitStack()
            x_sb = [scLN.enter_context(nc.sbuf_tensor(f"x_sb{i}", [P, D // 2],
                                                      F32)) for i in range(2)]
            xn_bf = scLN.enter_context(nc.sbuf_tensor("xn_bf", [P, D], BF16))
            stat = scLN.enter_context(nc.sbuf_tensor("stat", [P, 12], F32))

            def f(sy):
                sy.dma_start(out=tab[:, 0, :], in_=tabs_in[2]
                             ).then_inc(sem["dtab"], 16)
                sy.dma_start(out=tab[:, 1, :], in_=tabs_in[3]
                             ).then_inc(sem["dtab"], 16)
            em("sync", f)
            DTAB_K = bump("dtab", 2 * 16)

            ps_scope1 = ExitStack()
            ps_tr = ps_scope1.enter_context(
                nc.psum_tensor("ps_tr", [P, 4, P], BF16))
            ps_qkv = [ps_scope1.enter_context(
                nc.psum_tensor(f"ps_qkv{i}", [P, 512], F32)) for i in range(2)]
            ps_vtr = ps_scope1.enter_context(
                nc.psum_tensor("ps_vtr", [P, 4, P], BF16))

            tile_tr_pe = 0
            prev_var_dve = 0
            ln_act_half0 = 0
            for ti in range(NT):
                pv = prev_var_dve
                pa, pd, pp = n["act"], n["dve"], tile_tr_pe

                def f(sy, ti=ti, pa=pa, pd=pd):
                    if ti > 0:
                        sy.wait_ge(sem["act"], pa)
                        sy.wait_ge(sem["dve"], pd)
                    sy.dma_start(out=x_sb[0][:, :],
                                 in_=x_in[ti * P:(ti + 1) * P, 0:D // 2]
                                 ).then_inc(sem["dx"], 16)
                    sy.dma_start(out=x_sb[1][:, :],
                                 in_=x_in[ti * P:(ti + 1) * P, D // 2:D]
                                 ).then_inc(sem["dx"], 16)
                em("sync", f)
                dx_a = n["dx"] + 16
                dx_b = bump("dx", 32)

                def f(sc, dx_a=dx_a, dx_b=dx_b, pp=pp, ti=ti, pv=pv):
                    sc.wait_ge(sem["dx"], dx_a)
                    if ti > 0:
                        sc.wait_ge(sem["pe"], pp)
                        sc.wait_ge(sem["dve"], pv)
                    sc.activation(xn_bf[:, 0:D // 2], x_sb[0][:, :], AF.Square,
                                  accum_out=stat[:, 0:1]).then_inc(sem["act"], 1)
                    sc.wait_ge(sem["dx"], dx_b)
                    sc.activation(xn_bf[:, D // 2:D], x_sb[1][:, :], AF.Square,
                                  accum_out=stat[:, 8:9]).then_inc(sem["act"], 1)
                em("scalar", f)
                act_sq = bump("act", 2)

                c0 = n["dve"]

                def f(v, dx_a=dx_a, dx_b=dx_b, act_sq=act_sq, c0=c0):
                    v.wait_ge(sem["dx"], dx_a)
                    v.reduce_sum(stat[:, 1:2], x_sb[0][:, :], axis=AXX)
                    v.wait_ge(sem["dx"], dx_b)
                    v.reduce_sum(stat[:, 9:10], x_sb[1][:, :], axis=AXX
                                 ).then_inc(sem["dve"], 1)
                    v.wait_ge(sem["dve"], c0 + 1)
                    v.wait_ge(sem["act"], act_sq)
                    v.tensor_add(stat[:, 1:2], stat[:, 1:2], stat[:, 9:10])
                    v.tensor_add(stat[:, 0:1], stat[:, 0:1], stat[:, 8:9]
                                 ).then_inc(sem["dve"], 1)
                    v.wait_ge(sem["dve"], c0 + 2)
                    v.tensor_scalar_mul(stat[:, 2:3], stat[:, 1:2], 1.0 / D)
                    v.tensor_scalar_mul(stat[:, 3:4], stat[:, 0:1], 1.0 / D
                                        ).then_inc(sem["dve"], 1)
                    v.wait_ge(sem["dve"], c0 + 3)
                    v.tensor_mul(stat[:, 4:5], stat[:, 2:3], stat[:, 2:3]
                                 ).then_inc(sem["dve"], 1)
                    v.wait_ge(sem["dve"], c0 + 4)
                    v.tensor_scalar(out=stat[:, 5:6], in0=stat[:, 3:4],
                                    scalar1=stat[:, 4:5], scalar2=float(EPS),
                                    op0=ALU.subtract, op1=ALU.add
                                    ).then_inc(sem["dve"], 1)
                em("vector", f)
                dve_var = bump("dve", 5)

                def f(sc, dve_var=dve_var):
                    sc.wait_ge(sem["dve"], dve_var)
                    sc.activation(stat[:, 6:7], stat[:, 5:6], AF.Sqrt
                                  ).then_inc(sem["act"], 1)
                em("scalar", f)
                act_sqrt = bump("act")

                c1 = n["dve"]

                def f(v, act_sqrt=act_sqrt, c1=c1):
                    v.wait_ge(sem["act"], act_sqrt)
                    v.reciprocal(stat[:, 7:8], stat[:, 6:7]
                                 ).then_inc(sem["dve"], 1)
                    v.wait_ge(sem["dve"], c1 + 1)
                    v.scalar_tensor_tensor(out=stat[:, 4:5], in0=stat[:, 2:3],
                                           scalar=-1.0, in1=stat[:, 7:8],
                                           op0=ALU.mult, op1=ALU.mult
                                           ).then_inc(sem["dve"], 1)
                    v.wait_ge(sem["dve"], c1 + 2)
                    v.tensor_scalar(out=xn_bf[:, 0:D // 2], in0=x_sb[0][:, :],
                                    scalar1=stat[:, 7:8], scalar2=stat[:, 4:5],
                                    op0=ALU.mult, op1=ALU.add
                                    ).then_inc(sem["dve"], 1)
                    v.tensor_scalar(out=xn_bf[:, D // 2:D], in0=x_sb[1][:, :],
                                    scalar1=stat[:, 7:8], scalar2=stat[:, 4:5],
                                    op0=ALU.mult, op1=ALU.add
                                    ).then_inc(sem["dve"], 1)
                em("vector", f)
                dve_xn = bump("dve", 4)
                prev_var_dve = dve_var

                for g4 in range(8):
                    pv_act = n["act"]

                    def f(t, dve_xn=dve_xn, g4=g4, ti=ti, pv_act=pv_act):
                        t.wait_ge(sem["dve"], dve_xn - (1 if g4 < 4 else 0))
                        if ti == 0 and g4 == 0:
                            t.wait_ge(sem["pool"], POOL_CONST)
                        else:
                            t.wait_ge(sem["act"], pv_act)
                        for j in range(4):
                            dc = g4 * 4 + j
                            ins = t.transpose(ps_tr[:, j, :],
                                              xn_bf[:, dc * P:(dc + 1) * P],
                                              ident[:, :])
                            if j == 3:
                                ins.then_inc(sem["pe"], 1)
                    em("tensor", f)
                    pe_tr = bump("pe", 1)

                    def f(sc, pe_tr=pe_tr, g4=g4, ti=ti):
                        sc.wait_ge(sem["pe"], pe_tr)
                        sc.copy(xnT[:, g4 * 4:(g4 + 1) * 4,
                                    ti * P:(ti + 1) * P],
                                ps_tr[:, :, :]).then_inc(sem["act"], 1)
                    em("scalar", f)
                    bump("act")
                tile_tr_pe = n["pe"]
                if ti == 3:
                    ln_act_half0 = n["act"]
            LN_ACT = n["act"]
            LN_ACT_HALF = [ln_act_half0, LN_ACT]
            scLN.close()

            sched = ([("k", h) for h in range(NKVH)] +
                     [("v", h) for h in range(NKVH)] +
                     [("q", o) for o in range(NQH)])
            wsrc = {"k": wk, "v": wv, "q": wq}
            last_cast = 0
            chunk_pe = {}
            psum_reader = {}
            cont_dma = {}
            last_vtr_evac = 0
            kstg_guard = {}
            vstg_guard = {}

            for idx, (kind, oc) in enumerate(sched):
                wb = wbf[idx % 2]
                pc = chunk_pe.get(idx - 2, 0)

                def f(g, kind=kind, oc=oc, pc=pc, wb=wb):
                    if pc:
                        g.wait_ge(sem["pe"], pc)
                    g.dma_start(out=wb.ap().rearrange("p (a b) -> p a b", a=DC),
                                in_=wsrc[kind][oc]).then_inc(sem["dw"], 16)
                em("gpsimd", f)
                dw_chunk = bump("dw", 16)

                for tb in range(2):
                    prev_reader = psum_reader.get(tb, 0)

                    def f(t, tb=tb, dw_chunk=dw_chunk,
                          prev_reader=prev_reader, wb=wb):
                        t.wait_ge(sem["dw"], dw_chunk)
                        t.wait_ge(sem["act"], LN_ACT_HALF[tb])
                        if prev_reader:
                            t.wait_ge(sem["dve"], prev_reader)
                        for dc in range(DC):
                            ins = t.matmul(ps_qkv[tb][:, :],
                                           wb[:, dc * P:(dc + 1) * P],
                                           xnT[:, dc, tb * 512:(tb + 1) * 512],
                                           start=(dc == 0), stop=(dc == DC - 1))
                            if dc == DC - 1:
                                ins.then_inc(sem["pe"], 1)
                    em("tensor", f)
                    pe_mm = bump("pe", 1)

                    if kind in ("q", "k"):
                        kg = kstg_guard.get(tb, 0) if kind == "k" else 0

                        def f(v, kind=kind, oc=oc, tb=tb, pe_mm=pe_mm, kg=kg):
                            v.wait_ge(sem["pe"], pe_mm)
                            if kind == "k" and oc == 0 and tb == 0:
                                v.wait_ge(sem["dtab"], DTAB_K)
                            if kg:
                                v.wait_ge(sem["dcont"], kg)
                            ps = ps_qkv[tb]
                            ct = tab[:, 0, tb * 512:(tb + 1) * 512]
                            st = tab[:, 1, tb * 512:(tb + 1) * 512]
                            v.tensor_mul(ropet[0:64, :], ps[64:128, :],
                                         st[64:128, :])
                            v.tensor_mul(ropet[64:128, :], ps[0:64, :],
                                         st[0:64, :])
                            v.tensor_mul(ropec[:, :], ps[:, :], ct)
                            if kind == "q":
                                kvh, ql = oc // GRP, oc % GRP
                                v.tensor_add(
                                    QA[:, kvh, tb * 4:(tb + 1) * 4, ql, :],
                                    ropec.ap().rearrange("d (i t) -> d i t",
                                                         i=4),
                                    ropet.ap().rearrange("d (i t) -> d i t",
                                                         i=4)
                                    ).then_inc(sem["dve"], 1)
                            else:
                                v.tensor_add(kstg[tb][:, :], ropec[:, :],
                                             ropet[:, :]).then_inc(sem["dve"], 1)
                        em("vector", f)
                        psum_reader[tb] = bump("dve")

                        if kind == "k":
                            dve_k = n["dve"]

                            def f(g, oc=oc, tb=tb, dve_k=dve_k):
                                g.wait_ge(sem["dve"], dve_k)
                                g.dma_start(
                                    out=cc_in.ap()[oc * 2 + tb].rearrange(
                                        "(d t) -> d t", d=P),
                                    in_=kstg[tb][:, :]).then_inc(sem["dcont"], 16)
                            em("scalar", f)
                            kstg_guard[tb] = bump("dcont", 16)
                    else:
                        kg = kstg_guard.get(tb, 0)

                        def f(v, tb=tb, pe_mm=pe_mm, kg=kg):
                            v.wait_ge(sem["pe"], pe_mm)
                            if kg:
                                v.wait_ge(sem["dcont"], kg)
                            v.tensor_copy(kstg[tb][:, :], ps_qkv[tb][:, :]
                                          ).then_inc(sem["dve"], 1)
                        em("vector", f)
                        dve_v = bump("dve")
                        psum_reader[tb] = dve_v

                        lv = last_vtr_evac

                        def f(t, dve_v=dve_v, tb=tb, lv=lv):
                            t.wait_ge(sem["dve"], dve_v)
                            if lv:
                                t.wait_ge(sem["dve"], lv)
                            for i in range(4):
                                ins = t.transpose(ps_vtr[:, i, :],
                                                  kstg[tb][:, i * P:(i + 1) * P],
                                                  ident[:, :])
                                if i == 3:
                                    ins.then_inc(sem["pe"], 1)
                        em("tensor", f)
                        pe_vtr = bump("pe", 1)
                        kstg_guard[tb] = 0
                        psum_reader[tb] = max(psum_reader[tb], dve_v)

                        vg = vstg_guard.get(tb, 0)

                        def f(v, pe_vtr=pe_vtr, tb=tb, vg=vg):
                            v.wait_ge(sem["pe"], pe_vtr)
                            if vg:
                                v.wait_ge(sem["dcont"], vg)
                            v.tensor_copy(vstg[tb][:, :],
                                          ps_vtr.ap().rearrange(
                                              "t i d -> t (i d)")
                                          ).then_inc(sem["dve"], 1)
                        em("vector", f)
                        last_vtr_evac = bump("dve")
                        dve_vt = last_vtr_evac

                        def f(g, oc=oc, tb=tb, dve_vt=dve_vt):
                            g.wait_ge(sem["dve"], dve_vt)
                            g.dma_start(
                                out=cc_in.ap()[16 + oc * 2 + tb].rearrange(
                                    "(t f) -> t f", t=P),
                                in_=vstg[tb][:, :]).then_inc(sem["dcont"], 16)
                        em("scalar", f)
                        vstg_guard[tb] = bump("dcont", 16)

                        kstg_guard[tb] = 0
                chunk_pe[idx] = n["pe"]

                if kind == "k" and oc == NKVH - 1:
                    last_k_rope = n["dve"]

                    def f(sy, last_k_rope=last_k_rope):
                        sy.wait_ge(sem["dve"], last_k_rope)
                        sy.dma_start(out=tab[:, 0, :], in_=tabs_in[0]
                                     ).then_inc(sem["dtab"], 16)
                        sy.dma_start(out=tab[:, 1, :], in_=tabs_in[1]
                                     ).then_inc(sem["dtab"], 16)
                    em("sync", f)
                    DTAB_Q = bump("dtab", 32)

                    def f(v, DTAB_Q=DTAB_Q):
                        v.wait_ge(sem["dtab"], DTAB_Q)
                        v.nop().then_inc(sem["dve"], 1)
                    em("vector", f)
                    bump("dve")

            QKV_DVE = n["dve"]
            QKV_PE = n["pe"]
            DCONT_ALL = n["dcont"]

            def f(g):
                g.wait_ge(sem["dcont"], DCONT_ALL)
                g.collective_compute(
                    "AllGather", ALU.bypass,
                    replica_groups=[list(range(NC))],
                    ins=[cc_in.ap()], outs=[cc_out.ap()],
                ).then_inc(sem["cc"], 1)
            em("gpsimd", f)
            CC_DONE = bump("cc")

            if taps:
                def f(g, QKV_DVE=QKV_DVE):
                    g.wait_ge(sem["dve"], QKV_DVE)
                    g.wait_ge(sem["act"], LN_ACT)
                    g.dma_start(out=tap["xnT"],
                                in_=xnT.ap().rearrange(
                                    "p a b -> p (a b)").rearrange(
                                    "p (a c) -> p a c", c=128)
                                ).then_inc(sem["dtap"], 16)
                    g.dma_start(out=tap["qa1"],
                                in_=QA.ap().rearrange("p a b c d -> p (a b c d)")
                                ).then_inc(sem["dtap"], 16)
                    g.wait_ge(sem["cc"], CC_DONE)
                    g.dma_start(out=tap["cc"], in_=cc_out.ap()
                                ).then_inc(sem["dtap"], 16)
                em("gpsimd", f)
                bump("dtap", 48)
                TAP_QA1 = n["dtap"]

            ps_scope1.close()
            ps_scope2 = ExitStack()
            ps_qr = ps_scope2.enter_context(nc.psum_tensor("ps_qr", [16, 512], F32))
            ps_sr = ps_scope2.enter_context(
                nc.psum_tensor("ps_sr", [P, 4, EXP], F32))
            ps_gt = ps_scope2.enter_context(nc.psum_tensor("ps_gt", [EXP, P], BF16))
            ps_gx = ps_scope2.enter_context(nc.psum_tensor("ps_gx", [P, 512], F32))
            ps_za = [ps_scope2.enter_context(
                nc.psum_tensor(f"ps_za{i}", [P, 512], F32)) for i in range(2)]

            scFFN = ExitStack()
            FFA = scFFN.enter_context
            wqw_bf = FFA(nc.sbuf_tensor("wqw_bf", [P, DC, 16], BF16))
            gex = FFA(nc.sbuf_tensor("gex", [P, 4, NT * P], BF16))
            qr_sb = FFA(nc.sbuf_tensor("qr_sb", [16, NT * P], BF16))
            gt_sb = FFA(nc.sbuf_tensor("gt_sb", [EXP, P], BF16))
            m8 = FFA(nc.sbuf_tensor("m8", [P, 16], F32))
            srp = FFA(nc.sbuf_tensor("srp", [P, EXP], F32))
            srp2 = FFA(nc.sbuf_tensor("srp2", [P, EXP], F32))
            wgt = FFA(nc.sbuf_tensor("wgt", [P, EXP], F32))
            wgt16 = FFA(nc.sbuf_tensor("wgt16", [P, 16], F32))
            ge_sb = FFA(nc.sbuf_tensor("ge_sb", [P, EXP], F32))
            gsml = FFA(nc.sbuf_tensor("gsml", [P, 4], F32))
            gate_bf = FFA(nc.sbuf_tensor("gate_bf", [P, EXP], BF16))
            za_pe = {}

            def f(g):
                g.dma_start(out=keyst_bf[:, :], in_=keyst
                            ).then_inc(sem["dtab"], 16)
                g.dma_start(out=e2_bf[:, :], in_=e2).then_inc(sem["dtab"], 16)
                g.dma_start(out=wqw_bf[:, :, :], in_=wqw
                            ).then_inc(sem["dtab"], 16)
            em("gpsimd", f)
            DTAB_C = bump("dtab", 48)

            def f(v):
                v.wait_ge(sem["dtab"], DTAB_C)
                v.nop().then_inc(sem["dve"], 1)
            em("vector", f)
            DVE_CONSTS = bump("dve")

            qr_act = 0
            for tb in range(2):
                def f(t, tb=tb, qa=qr_act):
                    t.wait_ge(sem["dve"], max(DVE_CONSTS, QKV_DVE))
                    t.wait_ge(sem["act"], LN_ACT)
                    if tb == 1:
                        t.wait_ge(sem["act"], qa)
                    for dc in range(DC):
                        ins = t.matmul(ps_qr[:, :], wqw_bf[:, dc, :],
                                       xnT[:, dc, tb * 512:(tb + 1) * 512],
                                       start=(dc == 0), stop=(dc == DC - 1))
                        if dc == DC - 1:
                            ins.then_inc(sem["pe"], 1)
                em("tensor", f)
                pe_qr = bump("pe", 1)

                def f(sc, tb=tb, pe_qr=pe_qr):
                    sc.wait_ge(sem["pe"], pe_qr)
                    sc.copy(qr_sb[:, tb * 512:(tb + 1) * 512], ps_qr[:, :]
                            ).then_inc(sem["act"], 1)
                em("scalar", f)
                qr_act = bump("act")
            QR_ACT = qr_act

            last_round_dve = 0
            last_round_act = 0
            gate_dve = 0
            for rnd in range(2):
                lrd, lra = last_round_dve, last_round_act

                def f(t, lrd=lrd, lra=lra, rnd=rnd):
                    t.wait_ge(sem["act"], QR_ACT)
                    t.wait_ge(sem["dve"], DVE_CONSTS)
                    if lrd:
                        t.wait_ge(sem["dve"], lrd)
                        t.wait_ge(sem["act"], lra)
                    for j in range(4):
                        ti = rnd * 4 + j
                        ins = t.matmul(ps_sr[:, j, :],
                                       qr_sb[:, ti * P:(ti + 1) * P],
                                       keyst_bf[:, :], start=True, stop=True)
                        if j == 3:
                            ins.then_inc(sem["pe"], 1)
                em("tensor", f)
                pe_sr = bump("pe", 1)

                for j in range(4):
                    ti = rnd * 4 + j
                    c0 = n["dve"]

                    def f(v, j=j, pe_sr=pe_sr, c0=c0):
                        v.wait_ge(sem["pe"], pe_sr)
                        sr = ps_sr[:, j, :]
                        v.max(out=m8[:, 0:8], in_=sr).then_inc(sem["dve"], 1)
                        v.wait_ge(sem["dve"], c0 + 1)
                        v.match_replace(out=srp[:, :], in_to_replace=m8[:, 0:8],
                                        in_values=sr, imm_value=-1e30
                                        ).then_inc(sem["dve"], 1)
                        v.wait_ge(sem["dve"], c0 + 2)
                        v.max(out=m8[:, 8:16], in_=srp[:, :]
                              ).then_inc(sem["dve"], 1)
                        v.wait_ge(sem["dve"], c0 + 3)
                        v.match_replace(out=srp2[:, :],
                                        in_to_replace=m8[:, 8:16],
                                        in_values=srp[:, :], imm_value=-1e30
                                        ).then_inc(sem["dve"], 1)
                        v.wait_ge(sem["dve"], c0 + 4)
                        v.tensor_scalar_mul(gsml[:, 0:1], m8[:, 0:1], -1.0
                                            ).then_inc(sem["dve"], 1)
                    em("vector", f)
                    dve_g1 = bump("dve", 5)

                    def f(sc, j=j, dve_g1=dve_g1):
                        sc.wait_ge(sem["dve"], dve_g1)
                        sc.activation(wgt16[:, :], m8[:, :], AF.Exp,
                                      bias=gsml[:, 0:1],
                                      accum_out=gsml[:, 1:2])
                        sc.activation(wgt[:, :], ps_sr[:, j, :], AF.Exp,
                                      bias=gsml[:, 0:1]).then_inc(sem["act"], 1)
                    em("scalar", f)
                    act_g = bump("act")

                    c2 = n["dve"]

                    def f(v, j=j, act_g=act_g, c2=c2):
                        v.wait_ge(sem["act"], act_g)
                        v.tensor_tensor(ge_sb[:, :], ps_sr[:, j, :],
                                        srp2[:, :], ALU.is_gt)
                        v.reciprocal(gsml[:, 2:3], gsml[:, 1:2]
                                     ).then_inc(sem["dve"], 1)
                        v.wait_ge(sem["dve"], c2 + 1)
                        v.scalar_tensor_tensor(out=gate_bf[:, :], in0=wgt[:, :],
                                               scalar=gsml[:, 2:3],
                                               in1=ge_sb[:, :],
                                               op0=ALU.mult, op1=ALU.mult
                                               ).then_inc(sem["dve"], 1)
                    em("vector", f)
                    dve_gate = bump("dve", 2)

                    def f(t, dve_gate=dve_gate):
                        t.wait_ge(sem["dve"], dve_gate)
                        t.transpose(ps_gt[:, :], gate_bf[:, :], ident[:, :]
                                    ).then_inc(sem["pe"], 1)
                    em("tensor", f)
                    pe_gt = bump("pe")

                    def f(v, pe_gt=pe_gt):
                        v.wait_ge(sem["pe"], pe_gt)
                        v.tensor_copy(gt_sb[:, :], ps_gt[:, :]
                                      ).then_inc(sem["dve"], 1)
                    em("vector", f)
                    dve_gt = bump("dve")

                    def f(t, dve_gt=dve_gt):
                        t.wait_ge(sem["dve"], dve_gt)
                        for nc_ in range(4):
                            ins = t.matmul(ps_gx[:, nc_ * P:(nc_ + 1) * P],
                                           e2_bf[:, nc_ * P:(nc_ + 1) * P],
                                           gt_sb[:, :], start=True, stop=True)
                            if nc_ == 3:
                                ins.then_inc(sem["pe"], 1)
                    em("tensor", f)
                    pe_gx = bump("pe", 1)

                    def f(v, pe_gx=pe_gx, ti=ti):
                        v.wait_ge(sem["pe"], pe_gx)
                        v.tensor_copy(gex[:, :, ti * P:(ti + 1) * P],
                                      ps_gx.ap().rearrange("p (i t) -> p i t",
                                                           i=4)
                                      ).then_inc(sem["dve"], 1)
                    em("vector", f)
                    gate_dve = bump("dve")
                last_round_dve = gate_dve
                last_round_act = n["act"]
            GATES_DVE = gate_dve

            za_reader = {}
            for nc_ in range(4):
                wb = wbf[nc_ % 2]
                pc = za_pe.get(nc_ - 2, chunk_pe[len(sched) - 1]
                               if nc_ < 2 else 0)

                def f(g, nc_=nc_, pc=pc, wb=wb):
                    if pc:
                        g.wait_ge(sem["pe"], pc)
                    g.dma_start(out=wb.ap().rearrange("p (a b) -> p a b", a=DC),
                                in_=aw[nc_]).then_inc(sem["dw"], 16)
                em("gpsimd", f)
                cd = bump("dw", 16)

                for tb in range(2):
                    prev_rd = za_reader.get(tb, 0)

                    def f(t, cd=cd, tb=tb, prev_rd=prev_rd, wb=wb):
                        t.wait_ge(sem["dw"], cd)
                        if prev_rd:
                            t.wait_ge(sem["dve"], prev_rd)
                        for dc in range(DC):
                            ins = t.matmul(ps_za[tb][:, :],
                                           wb[:, dc * P:(dc + 1) * P],
                                           xnT[:, dc, tb * 512:(tb + 1) * 512],
                                           start=(dc == 0), stop=(dc == DC - 1))
                            if dc == DC - 1:
                                ins.then_inc(sem["pe"], 1)
                    em("tensor", f)
                    pe_za = bump("pe", 1)

                    def f(v, nc_=nc_, tb=tb, pe_za=pe_za):
                        v.wait_ge(sem["pe"], pe_za)
                        v.wait_ge(sem["dve"], GATES_DVE)
                        v.tensor_mul(zgT[:, nc_, tb * 512:(tb + 1) * 512],
                                     ps_za[tb][:, :],
                                     gex[:, nc_, tb * 512:(tb + 1) * 512]
                                     ).then_inc(sem["dve"], 1)
                    em("vector", f)
                    za_reader[tb] = bump("dve")
                za_pe[nc_] = n["pe"]
            if taps:
                def f(g, GATES_DVE=GATES_DVE, QR_ACT=QR_ACT, zz=n["dve"]):
                    g.wait_ge(sem["dve"], GATES_DVE)
                    g.dma_start(out=tap["gex"],
                                in_=gex.ap().rearrange("p a b -> p (a b)")
                                ).then_inc(sem["dtap"], 16)
                    g.wait_ge(sem["act"], QR_ACT)
                    g.dma_start(out=tap["qr"], in_=qr_sb[:, :]
                                ).then_inc(sem["dtap"], 16)
                    g.wait_ge(sem["dve"], zz)
                    g.dma_start(out=tap["zg"],
                                in_=zgT.ap().rearrange("p a b -> p (a b)")
                                ).then_inc(sem["dtap"], 16)
                em("gpsimd", f)
                bump("dtap", 48)

            ps_scope2.close()
            scFFN.close()

            A_DVE = n["dve"]
            A_PE = n["pe"]
            A_ACT = n["act"]

        with ExitStack() as scB1:
            EB = scB1.enter_context
            KT = [EB(nc.sbuf_tensor(f"KT{i}", [P, 64 * P], BF16))
                  for i in range(2)]
            VT = [EB(nc.sbuf_tensor(f"VT{i}", [P, 64 * P], BF16))
                  for i in range(2)]
            PT = [EB(nc.sbuf_tensor(f"PT{i}", [P, 1024], BF16)) for i in range(2)]
            pts_buf = [EB(nc.sbuf_tensor(f"pts{i}", [P, 512], BF16))
                       for i in range(2)]
            ptsum = [EB(nc.sbuf_tensor(f"ptsum{i}", [P, 512], F32))
                     for i in range(2)]
            ones_f32 = EB(nc.sbuf_tensor("ones_f32", [P, P], F32))
            lnl = [EB(nc.sbuf_tensor(f"lnl{i}", [P, 512], F32))
                   for i in range(2)]
            rec = [EB(nc.sbuf_tensor(f"rec{i}", [P, 512], F32))
                   for i in range(2)]
            masks = EB(nc.sbuf_tensor("masks_sb", [P, 8, GRP * P], BF16))

            ps_scope3 = ExitStack()
            ps_sc = [ps_scope3.enter_context(
                nc.psum_tensor(f"ps_sc{i}", [P, 2, 512], F32)) for i in range(2)]
            ps_av = [ps_scope3.enter_context(
                nc.psum_tensor(f"ps_av{i}", [P, 512], F32)) for i in range(2)]
            ps_lu = [ps_scope3.enter_context(
                nc.psum_tensor(f"ps_lu{i}", [P, 512], F32)) for i in range(2)]

            cc_k = cc_out.ap().rearrange("(c r) (d t) -> c d r t", c=NC, d=P)
            cc_v = cc_out.ap().rearrange("(c r) f -> c r f", c=NC)

            def f(sy):
                sy.wait_ge(sem["dve"], A_DVE)
                sy.wait_ge(sem["act"], A_ACT)
                sy.wait_ge(sem["pe"], A_PE)
                sy.dma_start(out=masks[:, :, :], in_=masks_in
                             ).then_inc(sem["dtab"], 16)
            em("sync", f)
            DTAB_M = bump("dtab", 16)

            def f(g):
                g.wait_ge(sem["dve"], A_DVE)
                g.wait_ge(sem["act"], A_ACT)
                g.wait_ge(sem["pe"], A_PE)
                g.memset(ones_f32[:, :], 1.0).then_inc(sem["pool"], 1)
            em("gpsimd", f)
            POOL_ONES = bump("pool")

            kv_loaded = []
            head_pe_done = [0] * NKVH

            def emit_kv_load(h):
                buf = h % 2
                pe_free = head_pe_done[h - 2] if h >= 2 else 0

                def f(sy, h=h, buf=buf, pe_free=pe_free):
                    sy.wait_ge(sem["cc"], CC_DONE)
                    if pe_free:
                        sy.wait_ge(sem["pe"], pe_free)
                    for oc in range(NC):
                        sy.dma_start(
                            out=KT[buf][:, oc * 1024:(oc + 1) * 1024]
                            .rearrange("d (r t) -> d r t", r=2),
                            in_=cc_k[oc, :, h * 2:h * 2 + 2, :]
                            ).then_inc(sem["dkv"], 16)
                        for r in range(2):
                            sy.dma_start(
                                out=VT[buf][:, (oc * 8 + r * 4) * P:
                                            (oc * 8 + r * 4 + 4) * P]
                                .rearrange("t (i d) -> t i d", i=4),
                                in_=cc_v[oc, 16 + h * 2 + r]
                                .rearrange("(t i d) -> t i d", t=P, i=4)
                                ).then_inc(sem["dkv"], 16)
                em("sync", f)
                kv_loaded.append(bump("dkv", 16 * NC * 3))

            emit_kv_load(0)
            emit_kv_load(1)

            pairs = []
            for h in range(NKVH):
                for j in range(NT):
                    ns = 8 * (j + 1)
                    for p in range(ns // 2):
                        pairs.append(dict(
                            h=h, j=j, unit=h * NT + j, p=p, npair=ns // 2,
                            s0=SLOT_OF[2 * p], s1=SLOT_OF[2 * p + 1],
                            masked=(2 * p >= ns - 8), r0=2 * p - (ns - 8)))
            NPAIR = len(pairs)
            PIPE = 2
            SKEW = 2
            exp_acts = [0] * NPAIR
            av_pes = [0] * NPAIR
            mask_dve = [0] * NPAIR
            pts_dve = [0] * NPAIR
            unit_lpe = []
            unit_norm = []
            unit_lrec = []
            first_att = True

            def emit_scores(i):
                pr = pairs[i]
                par = i % PIPE
                sc_wait = exp_acts[i - PIPE] if i >= PIPE else 0
                fa = first_att
                dkv_h = kv_loaded[pr["h"]]
                buf = pr["h"] % 2

                def f(t, pr=pr, par=par, sc_wait=sc_wait, fa=fa,
                      dkv_h=dkv_h, buf=buf):
                    if fa:
                        t.wait_ge(sem["dve"], A_DVE)
                        t.wait_ge(sem["act"], A_ACT)
                    if pr["p"] == 0:
                        t.wait_ge(sem["dkv"], dkv_h)
                    if sc_wait:
                        t.wait_ge(sem["act"], sc_wait)
                    t.matmul(ps_sc[par][:, 0, :],
                             KT[buf][:, pr["s0"] * P:(pr["s0"] + 1) * P],
                             QA[:, pr["h"], pr["j"], :, :],
                             start=True, stop=True)
                    t.matmul(ps_sc[par][:, 1, :],
                             KT[buf][:, pr["s1"] * P:(pr["s1"] + 1) * P],
                             QA[:, pr["h"], pr["j"], :, :],
                             start=True, stop=True).then_inc(sem["pe"], 1)
                em("tensor", f)
                pe_sc = bump("pe")

                pt_war_pe = av_pes[i - PIPE] if i >= PIPE else 0
                pt_war_dve = pts_dve[i - PIPE] if i >= PIPE else 0

                def f(sc, par=par, pe_sc=pe_sc, pt_war_pe=pt_war_pe,
                      pt_war_dve=pt_war_dve):
                    sc.wait_ge(sem["pe"], pe_sc)
                    if pt_war_pe:
                        sc.wait_ge(sem["pe"], pt_war_pe)
                    if pt_war_dve:
                        sc.wait_ge(sem["dve"], pt_war_dve)
                    sc.activation(PT[par][:, :],
                                  ps_sc[par].ap().rearrange("k a t -> k (a t)"),
                                  AF.Exp).then_inc(sem["act"], 1)
                em("scalar", f)
                exp_acts[i] = bump("act")

                if pr["masked"]:
                    def f(v, par=par, r0=pr["r0"], ea=exp_acts[i]):
                        v.wait_ge(sem["act"], ea)
                        v.wait_ge(sem["dtab"], DTAB_M)
                        v.tensor_mul(
                            PT[par][:, :], PT[par][:, :],
                            masks.ap().rearrange("k a t -> k (a t)")
                            [:, r0 * 512:(r0 + 2) * 512]).then_inc(sem["dve"], 1)
                    em("vector", f)
                    mask_dve[i] = bump("dve")

            def emit_av(i):
                pr = pairs[i]
                par = i % PIPE
                u = pr["unit"] % 2
                buf = pr["h"] % 2
                faw = unit_norm[pr["unit"] - 2] if pr["unit"] >= 2 else 0
                flw = unit_lrec[pr["unit"] - 2] if pr["unit"] >= 2 else 0

                def fv(v, par=par, ea=exp_acts[i], md=mask_dve[i]):
                    if not md:
                        v.wait_ge(sem["act"], ea)
                    v.tensor_add(pts_buf[par][:, :], PT[par][:, 0:512],
                                 PT[par][:, 512:1024]).then_inc(sem["dve"], 1)
                em("vector", fv)
                pts_dve[i] = bump("dve")

                def f(t, pr=pr, par=par, u=u, buf=buf, faw=faw, flw=flw,
                      ea=exp_acts[i], md=mask_dve[i], pd=pts_dve[i]):
                    if md:
                        t.wait_ge(sem["dve"], md)
                    else:
                        t.wait_ge(sem["act"], ea)
                    if pr["p"] == 0 and faw:
                        t.wait_ge(sem["dve"], faw)
                    last = pr["p"] == pr["npair"] - 1
                    t.matmul(ps_av[u][:, :],
                             VT[buf][:, pr["s0"] * P:(pr["s0"] + 1) * P],
                             PT[par][:, 0:512], start=(pr["p"] == 0),
                             stop=False)
                    t.matmul(ps_av[u][:, :],
                             VT[buf][:, pr["s1"] * P:(pr["s1"] + 1) * P],
                             PT[par][:, 512:1024], start=False, stop=last)
                    t.wait_ge(sem["dve"], pd)
                    if pr["p"] == 0 and flw:
                        t.wait_ge(sem["act"], flw)
                    t.matmul(ps_lu[u][:, :], ones_bf[:, :], pts_buf[par][:, :],
                             start=(pr["p"] == 0), stop=last
                             ).then_inc(sem["pe"], 1)
                em("tensor", f)
                av_pes[i] = bump("pe")

            def emit_tail(unit, i_last):
                u = unit % 2
                h, j = unit // NT, unit % NT
                rec_war = unit_norm[unit - 2] if unit >= 2 else 0

                c3 = n["act"]

                def f(sc, u=u, lp=av_pes[i_last], rec_war=rec_war, c3=c3):
                    sc.wait_ge(sem["pe"], lp)
                    if rec_war:
                        sc.wait_ge(sem["dve"], rec_war)
                    sc.activation(lnl[u][:, :], ps_lu[u][:, :], AF.Ln
                                  ).then_inc(sem["act"], 1)
                    sc.wait_ge(sem["act"], c3 + 1)
                    sc.activation(rec[u][:, :], lnl[u][:, :], AF.Exp,
                                  scale=-1.0).then_inc(sem["act"], 1)
                em("scalar", f)
                unit_lrec.append(bump("act", 2))

                def f(v, h=h, j=j, u=u, lr=unit_lrec[-1], ape=av_pes[i_last],
                      first=(unit == 0)):
                    v.wait_ge(sem["act"], lr)
                    v.wait_ge(sem["pe"], ape)
                    if taps and first:
                        v.wait_ge(sem["dtap"], TAP_QA1)
                    v.tensor_mul(
                        QA[:, h, j, :, :],
                        ps_av[u].ap().rearrange("d (i t) -> d i t", i=4),
                        rec[u].ap().rearrange("d (i t) -> d i t", i=4)
                        ).then_inc(sem["dve"], 1)
                em("vector", f)
                unit_norm.append(bump("dve"))

            for i in range(NPAIR + SKEW):
                if i >= SKEW:
                    emit_av(i - SKEW)
                    pr = pairs[i - SKEW]
                    if pr["p"] == pr["npair"] - 1:
                        emit_tail(pr["unit"], i - SKEW)
                        if pr["j"] == NT - 1:
                            h = pr["h"]
                            head_pe_done[h] = n["pe"]
                            if h + 2 < NKVH:
                                emit_kv_load(h + 2)
                if i < NPAIR:
                    emit_scores(i)
                    first_att = False

            if taps:
                def f(g, ad=unit_norm[-1]):
                    g.wait_ge(sem["dve"], ad)
                    g.dma_start(out=tap["qa2"],
                                in_=QA.ap().rearrange("p a b c d -> p (a b c d)")
                                ).then_inc(sem["dtap"], 16)
                em("gpsimd", f)
                bump("dtap", 16)

            ps_scope3.close()
            ATT_DVE = unit_norm[-1]
            ATT_ACT = unit_lrec[-1]
            ATT_PE = n["pe"]

        with ExitStack() as scB2:
            EB2 = scB2.enter_context
            wo_bf = [EB2(nc.sbuf_tensor(f"wo_bf{i}", [P, DC, 512], BF16))
                     for i in range(2)]
            b_bf = [EB2(nc.sbuf_tensor(f"b_bf{i}", [P, 4, 512], BF16))
                    for i in range(2)]
            x2 = [EB2(nc.sbuf_tensor(f"x2_{i}", [P, 512], F32))
                  for i in range(8)]
            out_sb = [EB2(nc.sbuf_tensor(f"out{i}", [P, 512], F32))
                      for i in range(4)]

            ps_scope4 = ExitStack()
            ps_fin = [ps_scope4.enter_context(
                nc.psum_tensor(f"ps_fin{i}", [P, 512], F32)) for i in range(4)]

            fin_pe = {}
            fin_dve = {}
            for hc in range(HC):
                pm = fin_pe.get(hc - 2, 0)

                def f(g, hc=hc, pm=pm):
                    if hc == 0:
                        g.wait_ge(sem["dve"], ATT_DVE)
                        g.wait_ge(sem["act"], ATT_ACT)
                        g.wait_ge(sem["pe"], ATT_PE)
                    if pm:
                        g.wait_ge(sem["pe"], pm)
                    g.dma_start(out=b_bf[hc % 2][:, :, :], in_=bw[hc]
                                ).then_inc(sem["dw"], 16)
                    g.dma_start(out=wo_bf[hc % 2][:, :, :], in_=wo[hc]
                                ).then_inc(sem["dw"], 16)
                em("gpsimd", f)
                dw_wo = bump("dw", 32)

                pd = fin_dve.get(hc - 1, 0)

                def f(sc, hc=hc, pd=pd):
                    if pd:
                        sc.wait_ge(sem["dve"], pd)
                    for ti in range(NT):
                        sc.dma_start(out=x2[ti][:, :],
                                     in_=x_in[ti * P:(ti + 1) * P,
                                              hc * 512:(hc + 1) * 512]
                                     ).then_inc(sem["dx2"], 16)
                em("scalar", f)
                dx2_here = bump("dx2", 16 * NT)

                for half in range(2):
                    prev_ev = fin_dve.get((hc, half - 1),
                                          fin_dve.get((hc - 1, 1), 0))

                    def f(t, hc=hc, half=half, dw_wo=dw_wo, prev_ev=prev_ev):
                        t.wait_ge(sem["dw"], dw_wo)
                        t.wait_ge(sem["dve"], ATT_DVE)
                        t.wait_ge(sem["act"], ATT_ACT)
                        if prev_ev:
                            t.wait_ge(sem["dve"], prev_ev)
                        for oc in range(NQH):
                            kvh, ql = oc // GRP, oc % GRP
                            for i in range(4):
                                ti = half * 4 + i
                                t.matmul(ps_fin[i][:, :],
                                         QA[:, kvh, ti, ql, :],
                                         wo_bf[hc % 2][:, oc, :],
                                         start=(oc == 0), stop=False)
                        for nc_ in range(4):
                            for i in range(4):
                                ti = half * 4 + i
                                ins = t.matmul(ps_fin[i][:, :],
                                               zgT[:, nc_, ti * P:(ti + 1) * P],
                                               b_bf[hc % 2][:, nc_, :],
                                               start=False, stop=(nc_ == 3))
                                if nc_ == 3 and i == 3:
                                    ins.then_inc(sem["pe"], 1)
                    em("tensor", f)
                    pe_fin = bump("pe", 1)
                    fin_pe[hc] = pe_fin

                    for i in range(4):
                        ti = half * 4 + i

                        def f(v, i=i, ti=ti, pe_fin=pe_fin, dx2_here=dx2_here):
                            v.wait_ge(sem["pe"], pe_fin)
                            v.wait_ge(sem["dx2"], dx2_here)
                            v.tensor_add(out_sb[i][:, :], ps_fin[i][:, :],
                                         x2[ti][:, :]).then_inc(sem["dve"], 1)
                        em("vector", f)
                        dve_out = bump("dve")
                        fin_dve[(hc, half)] = dve_out
                        fin_dve[hc] = dve_out

                        def f(sc, i=i, ti=ti, hc=hc, dve_out=dve_out):
                            sc.wait_ge(sem["dve"], dve_out)
                            sc.dma_start(out=y_out[ti * P:(ti + 1) * P,
                                                   hc * 512:(hc + 1) * 512],
                                         in_=out_sb[i][:, :]
                                         ).then_inc(sem["dout"], 16)
                        em("scalar", f)
                        bump("dout", 16)
            ps_scope4.close()
            DOUT_ALL = n["dout"]

            DTAP_ALL = n["dtap"]

            def f(sc):
                sc.wait_ge(sem["dout"], DOUT_ALL)
                if taps:
                    sc.wait_ge(sem["dtap"], DTAP_ALL)
            em("scalar", f)

        with nc.Block() as block:
            @block.sync
            def _(e):
                for eng, fn in plan:
                    if eng == "sync":
                        fn(e)

            @block.gpsimd
            def _(e):
                for eng, fn in plan:
                    if eng == "gpsimd":
                        fn(e)

            @block.vector
            def _(e):
                for eng, fn in plan:
                    if eng == "vector":
                        fn(e)

            @block.scalar
            def _(e):
                for eng, fn in plan:
                    if eng == "scalar":
                        fn(e)

            @block.tensor
            def _(e):
                for eng, fn in plan:
                    if eng == "tensor":
                        fn(e)

    return nc



def _tile_lhsT(w):
    O, Dd = w.shape
    a = np.ascontiguousarray(w.T).reshape(Dd // P, P, O // P, P)
    return np.ascontiguousarray(a.transpose(2, 1, 0, 3)).astype(np.float32)


def _tile_rhs(wt, ncol=512):
    K, H = wt.shape
    a = wt.reshape(K // P, P, H // ncol, ncol)
    return np.ascontiguousarray(a.transpose(2, 1, 0, 3)).astype(np.float32)


def _tables(tiles):
    pos = np.arange(S, dtype=np.float32)[:, None]
    freqs = np.exp(-np.arange(0, 128, 2, dtype=np.float32)
                   * np.float32(math.log(10000.0) / 128.0)).astype(np.float32)
    ang = (pos * freqs[None, :]).astype(np.float32)
    cos = np.cos(ang).astype(np.float32)
    sin = np.sin(ang).astype(np.float32)
    ct = np.concatenate([cos, cos], axis=1)
    st = np.concatenate([sin, -sin], axis=1)

    def per(tb, scale):
        sel = np.stack([tb[g * P:(g + 1) * P] for g in tiles], axis=0)
        sel = (sel * scale).astype(np.float32)
        return np.ascontiguousarray(
            sel.transpose(2, 0, 1).reshape(P, NT * P))
    return per(ct, SCL), per(st, SCL), per(ct, 1.0), per(st, 1.0)


def _masks(c):
    m = np.zeros((P, 8, GRP, P), dtype=np.float32)
    for r in range(8):
        if r < c:
            m[:, r] = 1.0
        elif r == c:
            tri = (np.arange(P)[None, :] >= np.arange(P)[:, None])
            m[:, r] = tri.astype(np.float32)[:, None, :]
    return np.ascontiguousarray(
        m.reshape(P, 8, GRP * P)).astype(ml_dtypes.bfloat16)


_CACHE = {}


def kernel(L0_attn_hidden_states, L0_attn_Wq, L0_attn_Wk, L0_attn_Wv,
           L0_attn_Wo, L0_ffn_A_w, L0_ffn_B_w, L0_ffn_Wq_w, L0_ffn_keys):
    x = np.ascontiguousarray(np.asarray(L0_attn_hidden_states,
                                        dtype=np.float32)[0])
    wq_t = _tile_lhsT(np.asarray(L0_attn_Wq, np.float32))
    wk_t = _tile_lhsT(np.asarray(L0_attn_Wk, np.float32))
    wv_t = _tile_lhsT(np.asarray(L0_attn_Wv, np.float32))
    aw_t = _tile_lhsT(np.asarray(L0_ffn_A_w, np.float32))
    wo_t = _tile_rhs(np.ascontiguousarray(np.asarray(L0_attn_Wo,
                                                     np.float32).T))
    bw_t = _tile_rhs(np.ascontiguousarray(np.asarray(L0_ffn_B_w,
                                                     np.float32).T))
    wqw_np = np.asarray(L0_ffn_Wq_w, np.float32)
    wqw_t = np.ascontiguousarray(
        wqw_np.T.reshape(DC, P, 16).transpose(1, 0, 2))
    keyst_np = np.ascontiguousarray(np.asarray(L0_ffn_keys, np.float32).T)
    e2_np = np.zeros((EXP, BOT), dtype=np.float32)
    for e in range(EXP):
        e2_np[e, e * 8:(e + 1) * 8] = 2.0

    if "nc" not in _CACHE:
        _CACHE["nc"] = build_nc()
    nc = _CACHE["nc"]

    in_maps = []
    for c in range(NC):
        tiles = [8 * j + c for j in range(NT)]
        xc = np.ascontiguousarray(
            np.concatenate([x[g * P:(g + 1) * P] for g in tiles], axis=0))
        tq_c, tq_s, tk_c, tk_s = _tables(tiles)
        tabs = np.ascontiguousarray(np.stack([tq_c, tq_s, tk_c, tk_s], axis=0))
        in_maps.append({
            "x_in": xc, "wq": wq_t, "wk": wk_t, "wv": wv_t, "wo": wo_t,
            "aw": aw_t, "bw": bw_t, "wqw": wqw_t, "keyst": keyst_np,
            "e2": e2_np, "tabs_in": tabs, "masks": _masks(c),
        })

    res = run_bass_kernel_spmd(nc, in_maps, core_ids=list(range(NC)))
    out = np.empty((S, D), dtype=np.float32)
    for c in range(NC):
        yc = res.results[c]["y_out"]
        for j in range(NT):
            g = 8 * j + c
            out[g * P:(g + 1) * P] = yc[j * P:(j + 1) * P]
    return out[None, :, :]


# version 20 (final)
# speedup vs baseline: 1.1908x; 1.1908x over previous
import math
from contextlib import ExitStack

import numpy as np
import ml_dtypes

import concourse.bass as bass
import concourse.mybir as mybir
from concourse.bass_utils import run_bass_kernel_spmd

F32 = mybir.dt.float32
BF16 = mybir.dt.bfloat16
AF = mybir.ActivationFunctionType
ALU = mybir.AluOpType
AXX = mybir.AxisListType.X

NC = 8
P = 128
S = 8192
D = 4096
NT = 8
NKVH = 8
GRP = 4
NQH = 32
DC = 32
EXP = 64
BOT = 512
HC = 8
EPS = 1e-5
SCL = 1.0 / math.sqrt(128.0)

SLOT_OF = [8 * (g % 8) + g // 8 for g in range(64)]


def build_nc(taps=False):
    nc = bass.Bass("TRN2", target_bir_lowering=False, debug=False)

    x_in = nc.dram_tensor("x_in", [NT * P, D], F32, kind="ExternalInput").ap()
    wq = nc.dram_tensor("wq", [NQH, P, DC, P], F32, kind="ExternalInput").ap()
    wk = nc.dram_tensor("wk", [NKVH, P, DC, P], F32, kind="ExternalInput").ap()
    wv = nc.dram_tensor("wv", [NKVH, P, DC, P], F32, kind="ExternalInput").ap()
    wo = nc.dram_tensor("wo", [HC, P, DC, 512], F32, kind="ExternalInput").ap()
    aw = nc.dram_tensor("aw", [4, P, DC, P], F32, kind="ExternalInput").ap()
    bw = nc.dram_tensor("bw", [HC, P, 4, 512], F32, kind="ExternalInput").ap()
    wqw = nc.dram_tensor("wqw", [P, DC, 16], F32, kind="ExternalInput").ap()
    keyst = nc.dram_tensor("keyst", [16, EXP], F32, kind="ExternalInput").ap()
    e2 = nc.dram_tensor("e2", [EXP, BOT], F32, kind="ExternalInput").ap()
    tabs_in = nc.dram_tensor("tabs_in", [4, P, NT * P], F32,
                             kind="ExternalInput").ap()
    masks_in = nc.dram_tensor("masks", [P, 8, GRP * P], BF16,
                              kind="ExternalInput").ap()
    y_out = nc.dram_tensor("y_out", [NT * P, D], F32, kind="ExternalOutput").ap()

    tap = {}
    if taps:
        tap["xnT"] = nc.dram_tensor("t_xnT", [P, DC * NT * P // 128, 128], BF16,
                                    kind="ExternalOutput").ap()
        tap["qa1"] = nc.dram_tensor("t_qa1", [P, NKVH * NT * GRP * P], BF16,
                                    kind="ExternalOutput").ap()
        tap["cc"] = nc.dram_tensor("t_cc", [256, 65536], BF16,
                                   kind="ExternalOutput").ap()
        tap["gex"] = nc.dram_tensor("t_gex", [P, 4 * NT * P], BF16,
                                    kind="ExternalOutput").ap()
        tap["qr"] = nc.dram_tensor("t_qr", [16, NT * P], BF16,
                                   kind="ExternalOutput").ap()
        tap["zg"] = nc.dram_tensor("t_zg", [P, 4 * NT * P], BF16,
                                   kind="ExternalOutput").ap()
        tap["qa2"] = nc.dram_tensor("t_qa2", [P, NKVH * NT * GRP * P], BF16,
                                    kind="ExternalOutput").ap()

    cc_in = nc.dram_tensor("cc_in", [32, 65536], BF16)
    cc_out = nc.dram_tensor("cc_out", [256, 65536], BF16, addr_space="Shared")

    plan = []

    def em(engine, fn):
        plan.append((engine, fn))

    n = dict(pe=0, dve=0, act=0, pool=0, cc=0,
             dw=0, dx=0, dtab=0, dcont=0, dkv=0, dx2=0, dout=0, dtap=0)

    def bump(key, amt=1):
        n[key] += amt
        return n[key]

    with ExitStack() as top:
        E = top.enter_context
        sem = {k: E(nc.semaphore(k)) for k in n}

        QA = E(nc.sbuf_tensor("QA", [P, NKVH, NT, GRP, P], BF16))
        zgT = E(nc.sbuf_tensor("zgT", [P, 4, NT * P], BF16))
        ident = E(nc.sbuf_tensor("ident", [P, P], BF16))
        ones_bf = E(nc.sbuf_tensor("ones_bf", [P, P], BF16))
        keyst_bf = E(nc.sbuf_tensor("keyst_bf", [16, EXP], BF16))
        e2_bf = E(nc.sbuf_tensor("e2_bf", [EXP, BOT], BF16))

        def f(g):
            g.memset(ident[:, :], 0.0)
            g.affine_select(out=ident[:, :], in_=ident[:, :],
                            compare_op=ALU.not_equal, fill=1.0, base=0,
                            pattern=[[-1, P]], channel_multiplier=1)
            g.memset(ones_bf[:, :], 1.0).then_inc(sem["pool"], 1)
        em("gpsimd", f)
        POOL_CONST = bump("pool")

        with ExitStack() as scA:
            EA = scA.enter_context
            xnT = EA(nc.sbuf_tensor("xnT", [P, DC, NT * P], BF16))
            tab = EA(nc.sbuf_tensor("tab", [P, 2, NT * P], F32))
            wbf = [EA(nc.sbuf_tensor(f"wbf{i}", [P, D], BF16)) for i in range(2)]
            ropet = EA(nc.sbuf_tensor("ropet", [P, 512], F32))
            ropec = EA(nc.sbuf_tensor("ropec", [P, 512], F32))
            kstg = [EA(nc.sbuf_tensor(f"kstg{i}", [P, 512], BF16))
                    for i in range(2)]
            vstg = [EA(nc.sbuf_tensor(f"vstg{i}", [P, 512], BF16))
                    for i in range(2)]
            scLN = ExitStack()
            x_sb = [scLN.enter_context(nc.sbuf_tensor(f"x_sb{i}", [P, D // 2],
                                                      F32)) for i in range(2)]
            xn_bf = scLN.enter_context(nc.sbuf_tensor("xn_bf", [P, D], BF16))
            stat = scLN.enter_context(nc.sbuf_tensor("stat", [P, 12], F32))

            def f(sy):
                sy.dma_start(out=tab[:, 0, :], in_=tabs_in[2]
                             ).then_inc(sem["dtab"], 16)
                sy.dma_start(out=tab[:, 1, :], in_=tabs_in[3]
                             ).then_inc(sem["dtab"], 16)
            em("sync", f)
            DTAB_K = bump("dtab", 2 * 16)

            ps_scope1 = ExitStack()
            ps_tr = ps_scope1.enter_context(
                nc.psum_tensor("ps_tr", [P, 4, P], BF16))
            ps_qkv = [ps_scope1.enter_context(
                nc.psum_tensor(f"ps_qkv{i}", [P, 512], F32)) for i in range(2)]
            ps_vtr = ps_scope1.enter_context(
                nc.psum_tensor("ps_vtr", [P, 4, P], BF16))

            tile_tr_pe = 0
            prev_var_dve = 0
            ln_act_half0 = 0
            for ti in range(NT):
                pv = prev_var_dve
                pa, pd, pp = n["act"], n["dve"], tile_tr_pe

                def f(sy, ti=ti, pa=pa, pd=pd):
                    if ti > 0:
                        sy.wait_ge(sem["act"], pa)
                        sy.wait_ge(sem["dve"], pd)
                    sy.dma_start(out=x_sb[0][:, :],
                                 in_=x_in[ti * P:(ti + 1) * P, 0:D // 2]
                                 ).then_inc(sem["dx"], 16)
                    sy.dma_start(out=x_sb[1][:, :],
                                 in_=x_in[ti * P:(ti + 1) * P, D // 2:D]
                                 ).then_inc(sem["dx"], 16)
                em("sync", f)
                dx_a = n["dx"] + 16
                dx_b = bump("dx", 32)

                def f(sc, dx_a=dx_a, dx_b=dx_b, pp=pp, ti=ti, pv=pv):
                    sc.wait_ge(sem["dx"], dx_a)
                    if ti > 0:
                        sc.wait_ge(sem["pe"], pp)
                        sc.wait_ge(sem["dve"], pv)
                    sc.activation(xn_bf[:, 0:D // 2], x_sb[0][:, :], AF.Square,
                                  accum_out=stat[:, 0:1]).then_inc(sem["act"], 1)
                    sc.wait_ge(sem["dx"], dx_b)
                    sc.activation(xn_bf[:, D // 2:D], x_sb[1][:, :], AF.Square,
                                  accum_out=stat[:, 8:9]).then_inc(sem["act"], 1)
                em("scalar", f)
                act_sq = bump("act", 2)

                c0 = n["dve"]

                def f(v, dx_a=dx_a, dx_b=dx_b, act_sq=act_sq, c0=c0):
                    v.wait_ge(sem["dx"], dx_a)
                    v.reduce_sum(stat[:, 1:2], x_sb[0][:, :], axis=AXX)
                    v.wait_ge(sem["dx"], dx_b)
                    v.reduce_sum(stat[:, 9:10], x_sb[1][:, :], axis=AXX
                                 ).then_inc(sem["dve"], 1)
                    v.wait_ge(sem["dve"], c0 + 1)
                    v.wait_ge(sem["act"], act_sq)
                    v.tensor_add(stat[:, 1:2], stat[:, 1:2], stat[:, 9:10])
                    v.tensor_add(stat[:, 0:1], stat[:, 0:1], stat[:, 8:9]
                                 ).then_inc(sem["dve"], 1)
                    v.wait_ge(sem["dve"], c0 + 2)
                    v.tensor_scalar_mul(stat[:, 2:3], stat[:, 1:2], 1.0 / D)
                    v.tensor_scalar_mul(stat[:, 3:4], stat[:, 0:1], 1.0 / D
                                        ).then_inc(sem["dve"], 1)
                    v.wait_ge(sem["dve"], c0 + 3)
                    v.tensor_mul(stat[:, 4:5], stat[:, 2:3], stat[:, 2:3]
                                 ).then_inc(sem["dve"], 1)
                    v.wait_ge(sem["dve"], c0 + 4)
                    v.tensor_scalar(out=stat[:, 5:6], in0=stat[:, 3:4],
                                    scalar1=stat[:, 4:5], scalar2=float(EPS),
                                    op0=ALU.subtract, op1=ALU.add
                                    ).then_inc(sem["dve"], 1)
                em("vector", f)
                dve_var = bump("dve", 5)

                def f(sc, dve_var=dve_var):
                    sc.wait_ge(sem["dve"], dve_var)
                    sc.activation(stat[:, 6:7], stat[:, 5:6], AF.Sqrt
                                  ).then_inc(sem["act"], 1)
                em("scalar", f)
                act_sqrt = bump("act")

                c1 = n["dve"]

                def f(v, act_sqrt=act_sqrt, c1=c1):
                    v.wait_ge(sem["act"], act_sqrt)
                    v.reciprocal(stat[:, 7:8], stat[:, 6:7]
                                 ).then_inc(sem["dve"], 1)
                    v.wait_ge(sem["dve"], c1 + 1)
                    v.scalar_tensor_tensor(out=stat[:, 4:5], in0=stat[:, 2:3],
                                           scalar=-1.0, in1=stat[:, 7:8],
                                           op0=ALU.mult, op1=ALU.mult
                                           ).then_inc(sem["dve"], 1)
                    v.wait_ge(sem["dve"], c1 + 2)
                    v.tensor_scalar(out=xn_bf[:, 0:D // 2], in0=x_sb[0][:, :],
                                    scalar1=stat[:, 7:8], scalar2=stat[:, 4:5],
                                    op0=ALU.mult, op1=ALU.add
                                    ).then_inc(sem["dve"], 1)
                    v.tensor_scalar(out=xn_bf[:, D // 2:D], in0=x_sb[1][:, :],
                                    scalar1=stat[:, 7:8], scalar2=stat[:, 4:5],
                                    op0=ALU.mult, op1=ALU.add
                                    ).then_inc(sem["dve"], 1)
                em("vector", f)
                dve_xn = bump("dve", 4)
                prev_var_dve = dve_var

                for g4 in range(8):
                    pv_act = n["act"]

                    def f(t, dve_xn=dve_xn, g4=g4, ti=ti, pv_act=pv_act):
                        t.wait_ge(sem["dve"], dve_xn - (1 if g4 < 4 else 0))
                        if ti == 0 and g4 == 0:
                            t.wait_ge(sem["pool"], POOL_CONST)
                        else:
                            t.wait_ge(sem["act"], pv_act)
                        for j in range(4):
                            dc = g4 * 4 + j
                            ins = t.transpose(ps_tr[:, j, :],
                                              xn_bf[:, dc * P:(dc + 1) * P],
                                              ident[:, :])
                            if j == 3:
                                ins.then_inc(sem["pe"], 1)
                    em("tensor", f)
                    pe_tr = bump("pe", 1)

                    def f(sc, pe_tr=pe_tr, g4=g4, ti=ti):
                        sc.wait_ge(sem["pe"], pe_tr)
                        sc.copy(xnT[:, g4 * 4:(g4 + 1) * 4,
                                    ti * P:(ti + 1) * P],
                                ps_tr[:, :, :]).then_inc(sem["act"], 1)
                    em("scalar", f)
                    bump("act")
                tile_tr_pe = n["pe"]
                if ti == 3:
                    ln_act_half0 = n["act"]
            LN_ACT = n["act"]
            LN_ACT_HALF = [ln_act_half0, LN_ACT]
            scLN.close()

            sched = ([("k", h) for h in range(NKVH)] +
                     [("v", h) for h in range(NKVH)] +
                     [("q", o) for o in range(NQH)])
            wsrc = {"k": wk, "v": wv, "q": wq}
            last_cast = 0
            chunk_pe = {}
            psum_reader = {}
            cont_dma = {}
            last_vtr_evac = 0
            kstg_guard = {}
            vstg_guard = {}

            for idx, (kind, oc) in enumerate(sched):
                wb = wbf[idx % 2]
                pc = chunk_pe.get(idx - 2, 0)

                def f(g, kind=kind, oc=oc, pc=pc, wb=wb):
                    if pc:
                        g.wait_ge(sem["pe"], pc)
                    g.dma_start(out=wb.ap().rearrange("p (a b) -> p a b", a=DC),
                                in_=wsrc[kind][oc]).then_inc(sem["dw"], 16)
                em("gpsimd", f)
                dw_chunk = bump("dw", 16)

                for tb in range(2):
                    prev_reader = psum_reader.get(tb, 0)

                    def f(t, tb=tb, dw_chunk=dw_chunk,
                          prev_reader=prev_reader, wb=wb):
                        t.wait_ge(sem["dw"], dw_chunk)
                        t.wait_ge(sem["act"], LN_ACT_HALF[tb])
                        if prev_reader:
                            t.wait_ge(sem["dve"], prev_reader)
                        for dc in range(DC):
                            ins = t.matmul(ps_qkv[tb][:, :],
                                           wb[:, dc * P:(dc + 1) * P],
                                           xnT[:, dc, tb * 512:(tb + 1) * 512],
                                           start=(dc == 0), stop=(dc == DC - 1))
                            if dc == DC - 1:
                                ins.then_inc(sem["pe"], 1)
                    em("tensor", f)
                    pe_mm = bump("pe", 1)

                    if kind in ("q", "k"):
                        kg = kstg_guard.get(tb, 0) if kind == "k" else 0

                        def f(v, kind=kind, oc=oc, tb=tb, pe_mm=pe_mm, kg=kg):
                            v.wait_ge(sem["pe"], pe_mm)
                            if kind == "k" and oc == 0 and tb == 0:
                                v.wait_ge(sem["dtab"], DTAB_K)
                            if kg:
                                v.wait_ge(sem["dcont"], kg)
                            ps = ps_qkv[tb]
                            ct = tab[:, 0, tb * 512:(tb + 1) * 512]
                            st = tab[:, 1, tb * 512:(tb + 1) * 512]
                            v.tensor_mul(ropet[0:64, :], ps[64:128, :],
                                         st[64:128, :])
                            v.tensor_mul(ropet[64:128, :], ps[0:64, :],
                                         st[0:64, :])
                            v.tensor_mul(ropec[:, :], ps[:, :], ct)
                            if kind == "q":
                                kvh, ql = oc // GRP, oc % GRP
                                v.tensor_add(
                                    QA[:, kvh, tb * 4:(tb + 1) * 4, ql, :],
                                    ropec.ap().rearrange("d (i t) -> d i t",
                                                         i=4),
                                    ropet.ap().rearrange("d (i t) -> d i t",
                                                         i=4)
                                    ).then_inc(sem["dve"], 1)
                            else:
                                v.tensor_add(kstg[tb][:, :], ropec[:, :],
                                             ropet[:, :]).then_inc(sem["dve"], 1)
                        em("vector", f)
                        psum_reader[tb] = bump("dve")

                        if kind == "k":
                            dve_k = n["dve"]

                            def f(g, oc=oc, tb=tb, dve_k=dve_k):
                                g.wait_ge(sem["dve"], dve_k)
                                g.dma_start(
                                    out=cc_in.ap()[oc * 2 + tb].rearrange(
                                        "(d t) -> d t", d=P),
                                    in_=kstg[tb][:, :]).then_inc(sem["dcont"], 16)
                            em("scalar", f)
                            kstg_guard[tb] = bump("dcont", 16)
                    else:
                        kg = kstg_guard.get(tb, 0)

                        def f(v, tb=tb, pe_mm=pe_mm, kg=kg):
                            v.wait_ge(sem["pe"], pe_mm)
                            if kg:
                                v.wait_ge(sem["dcont"], kg)
                            v.tensor_copy(kstg[tb][:, :], ps_qkv[tb][:, :]
                                          ).then_inc(sem["dve"], 1)
                        em("vector", f)
                        dve_v = bump("dve")
                        psum_reader[tb] = dve_v

                        lv = last_vtr_evac

                        def f(t, dve_v=dve_v, tb=tb, lv=lv):
                            t.wait_ge(sem["dve"], dve_v)
                            if lv:
                                t.wait_ge(sem["dve"], lv)
                            for i in range(4):
                                ins = t.transpose(ps_vtr[:, i, :],
                                                  kstg[tb][:, i * P:(i + 1) * P],
                                                  ident[:, :])
                                if i == 3:
                                    ins.then_inc(sem["pe"], 1)
                        em("tensor", f)
                        pe_vtr = bump("pe", 1)
                        kstg_guard[tb] = 0
                        psum_reader[tb] = max(psum_reader[tb], dve_v)

                        vg = vstg_guard.get(tb, 0)

                        def f(v, pe_vtr=pe_vtr, tb=tb, vg=vg):
                            v.wait_ge(sem["pe"], pe_vtr)
                            if vg:
                                v.wait_ge(sem["dcont"], vg)
                            v.tensor_copy(vstg[tb][:, :],
                                          ps_vtr.ap().rearrange(
                                              "t i d -> t (i d)")
                                          ).then_inc(sem["dve"], 1)
                        em("vector", f)
                        last_vtr_evac = bump("dve")
                        dve_vt = last_vtr_evac

                        def f(g, oc=oc, tb=tb, dve_vt=dve_vt):
                            g.wait_ge(sem["dve"], dve_vt)
                            g.dma_start(
                                out=cc_in.ap()[16 + oc * 2 + tb].rearrange(
                                    "(t f) -> t f", t=P),
                                in_=vstg[tb][:, :]).then_inc(sem["dcont"], 16)
                        em("scalar", f)
                        vstg_guard[tb] = bump("dcont", 16)

                        kstg_guard[tb] = 0
                chunk_pe[idx] = n["pe"]

                if kind == "k" and oc == NKVH - 1:
                    last_k_rope = n["dve"]

                    def f(sy, last_k_rope=last_k_rope):
                        sy.wait_ge(sem["dve"], last_k_rope)
                        sy.dma_start(out=tab[:, 0, :], in_=tabs_in[0]
                                     ).then_inc(sem["dtab"], 16)
                        sy.dma_start(out=tab[:, 1, :], in_=tabs_in[1]
                                     ).then_inc(sem["dtab"], 16)
                    em("sync", f)
                    DTAB_Q = bump("dtab", 32)

                    def f(v, DTAB_Q=DTAB_Q):
                        v.wait_ge(sem["dtab"], DTAB_Q)
                        v.nop().then_inc(sem["dve"], 1)
                    em("vector", f)
                    bump("dve")

            QKV_DVE = n["dve"]
            QKV_PE = n["pe"]
            DCONT_ALL = n["dcont"]

            def f(g):
                g.wait_ge(sem["dcont"], DCONT_ALL)
                g.collective_compute(
                    "AllGather", ALU.bypass,
                    replica_groups=[list(range(NC))],
                    ins=[cc_in.ap()], outs=[cc_out.ap()],
                ).then_inc(sem["cc"], 1)
            em("gpsimd", f)
            CC_DONE = bump("cc")

            if taps:
                def f(g, QKV_DVE=QKV_DVE):
                    g.wait_ge(sem["dve"], QKV_DVE)
                    g.wait_ge(sem["act"], LN_ACT)
                    g.dma_start(out=tap["xnT"],
                                in_=xnT.ap().rearrange(
                                    "p a b -> p (a b)").rearrange(
                                    "p (a c) -> p a c", c=128)
                                ).then_inc(sem["dtap"], 16)
                    g.dma_start(out=tap["qa1"],
                                in_=QA.ap().rearrange("p a b c d -> p (a b c d)")
                                ).then_inc(sem["dtap"], 16)
                    g.wait_ge(sem["cc"], CC_DONE)
                    g.dma_start(out=tap["cc"], in_=cc_out.ap()
                                ).then_inc(sem["dtap"], 16)
                em("gpsimd", f)
                bump("dtap", 48)
                TAP_QA1 = n["dtap"]

            ps_scope1.close()
            ps_scope2 = ExitStack()
            ps_qr = ps_scope2.enter_context(nc.psum_tensor("ps_qr", [16, 512], F32))
            ps_sr = ps_scope2.enter_context(
                nc.psum_tensor("ps_sr", [P, 4, EXP], F32))
            ps_gt = ps_scope2.enter_context(nc.psum_tensor("ps_gt", [EXP, P], BF16))
            ps_gx = ps_scope2.enter_context(nc.psum_tensor("ps_gx", [P, 512], F32))
            ps_za = [ps_scope2.enter_context(
                nc.psum_tensor(f"ps_za{i}", [P, 512], F32)) for i in range(2)]

            scFFN = ExitStack()
            FFA = scFFN.enter_context
            wqw_bf = FFA(nc.sbuf_tensor("wqw_bf", [P, DC, 16], BF16))
            gex = FFA(nc.sbuf_tensor("gex", [P, 4, NT * P], BF16))
            qr_sb = FFA(nc.sbuf_tensor("qr_sb", [16, NT * P], BF16))
            gt_sb = FFA(nc.sbuf_tensor("gt_sb", [EXP, P], BF16))
            m8 = FFA(nc.sbuf_tensor("m8", [P, 16], F32))
            srp = FFA(nc.sbuf_tensor("srp", [P, EXP], F32))
            srp2 = FFA(nc.sbuf_tensor("srp2", [P, EXP], F32))
            wgt = FFA(nc.sbuf_tensor("wgt", [P, EXP], F32))
            wgt16 = FFA(nc.sbuf_tensor("wgt16", [P, 16], F32))
            ge_sb = FFA(nc.sbuf_tensor("ge_sb", [P, EXP], F32))
            gsml = FFA(nc.sbuf_tensor("gsml", [P, 4], F32))
            gate_bf = FFA(nc.sbuf_tensor("gate_bf", [P, EXP], BF16))
            za_pe = {}

            def f(g):
                g.dma_start(out=keyst_bf[:, :], in_=keyst
                            ).then_inc(sem["dtab"], 16)
                g.dma_start(out=e2_bf[:, :], in_=e2).then_inc(sem["dtab"], 16)
                g.dma_start(out=wqw_bf[:, :, :], in_=wqw
                            ).then_inc(sem["dtab"], 16)
            em("gpsimd", f)
            DTAB_C = bump("dtab", 48)

            def f(v):
                v.wait_ge(sem["dtab"], DTAB_C)
                v.nop().then_inc(sem["dve"], 1)
            em("vector", f)
            DVE_CONSTS = bump("dve")

            qr_act = 0
            for tb in range(2):
                def f(t, tb=tb, qa=qr_act):
                    t.wait_ge(sem["dve"], max(DVE_CONSTS, QKV_DVE))
                    t.wait_ge(sem["act"], LN_ACT)
                    if tb == 1:
                        t.wait_ge(sem["act"], qa)
                    for dc in range(DC):
                        ins = t.matmul(ps_qr[:, :], wqw_bf[:, dc, :],
                                       xnT[:, dc, tb * 512:(tb + 1) * 512],
                                       start=(dc == 0), stop=(dc == DC - 1))
                        if dc == DC - 1:
                            ins.then_inc(sem["pe"], 1)
                em("tensor", f)
                pe_qr = bump("pe", 1)

                def f(sc, tb=tb, pe_qr=pe_qr):
                    sc.wait_ge(sem["pe"], pe_qr)
                    sc.copy(qr_sb[:, tb * 512:(tb + 1) * 512], ps_qr[:, :]
                            ).then_inc(sem["act"], 1)
                em("scalar", f)
                qr_act = bump("act")
            QR_ACT = qr_act

            last_round_dve = 0
            last_round_act = 0
            gate_dve = 0
            for rnd in range(2):
                lrd, lra = last_round_dve, last_round_act

                def f(t, lrd=lrd, lra=lra, rnd=rnd):
                    t.wait_ge(sem["act"], QR_ACT)
                    t.wait_ge(sem["dve"], DVE_CONSTS)
                    if lrd:
                        t.wait_ge(sem["dve"], lrd)
                        t.wait_ge(sem["act"], lra)
                    for j in range(4):
                        ti = rnd * 4 + j
                        ins = t.matmul(ps_sr[:, j, :],
                                       qr_sb[:, ti * P:(ti + 1) * P],
                                       keyst_bf[:, :], start=True, stop=True)
                        if j == 3:
                            ins.then_inc(sem["pe"], 1)
                em("tensor", f)
                pe_sr = bump("pe", 1)

                for j in range(4):
                    ti = rnd * 4 + j
                    c0 = n["dve"]

                    def f(v, j=j, pe_sr=pe_sr, c0=c0):
                        v.wait_ge(sem["pe"], pe_sr)
                        sr = ps_sr[:, j, :]
                        v.max(out=m8[:, 0:8], in_=sr).then_inc(sem["dve"], 1)
                        v.wait_ge(sem["dve"], c0 + 1)
                        v.match_replace(out=srp[:, :], in_to_replace=m8[:, 0:8],
                                        in_values=sr, imm_value=-1e30
                                        ).then_inc(sem["dve"], 1)
                        v.wait_ge(sem["dve"], c0 + 2)
                        v.max(out=m8[:, 8:16], in_=srp[:, :]
                              ).then_inc(sem["dve"], 1)
                        v.wait_ge(sem["dve"], c0 + 3)
                        v.match_replace(out=srp2[:, :],
                                        in_to_replace=m8[:, 8:16],
                                        in_values=srp[:, :], imm_value=-1e30
                                        ).then_inc(sem["dve"], 1)
                        v.wait_ge(sem["dve"], c0 + 4)
                        v.tensor_scalar_mul(gsml[:, 0:1], m8[:, 0:1], -1.0
                                            ).then_inc(sem["dve"], 1)
                    em("vector", f)
                    dve_g1 = bump("dve", 5)

                    def f(sc, j=j, dve_g1=dve_g1):
                        sc.wait_ge(sem["dve"], dve_g1)
                        sc.activation(wgt16[:, :], m8[:, :], AF.Exp,
                                      bias=gsml[:, 0:1],
                                      accum_out=gsml[:, 1:2])
                        sc.activation(wgt[:, :], ps_sr[:, j, :], AF.Exp,
                                      bias=gsml[:, 0:1]).then_inc(sem["act"], 1)
                    em("scalar", f)
                    act_g = bump("act")

                    c2 = n["dve"]

                    def f(v, j=j, act_g=act_g, c2=c2):
                        v.wait_ge(sem["act"], act_g)
                        v.tensor_tensor(ge_sb[:, :], ps_sr[:, j, :],
                                        srp2[:, :], ALU.is_gt)
                        v.reciprocal(gsml[:, 2:3], gsml[:, 1:2]
                                     ).then_inc(sem["dve"], 1)
                        v.wait_ge(sem["dve"], c2 + 1)
                        v.scalar_tensor_tensor(out=gate_bf[:, :], in0=wgt[:, :],
                                               scalar=gsml[:, 2:3],
                                               in1=ge_sb[:, :],
                                               op0=ALU.mult, op1=ALU.mult
                                               ).then_inc(sem["dve"], 1)
                    em("vector", f)
                    dve_gate = bump("dve", 2)

                    def f(t, dve_gate=dve_gate):
                        t.wait_ge(sem["dve"], dve_gate)
                        t.transpose(ps_gt[:, :], gate_bf[:, :], ident[:, :]
                                    ).then_inc(sem["pe"], 1)
                    em("tensor", f)
                    pe_gt = bump("pe")

                    def f(v, pe_gt=pe_gt):
                        v.wait_ge(sem["pe"], pe_gt)
                        v.tensor_copy(gt_sb[:, :], ps_gt[:, :]
                                      ).then_inc(sem["dve"], 1)
                    em("vector", f)
                    dve_gt = bump("dve")

                    def f(t, dve_gt=dve_gt):
                        t.wait_ge(sem["dve"], dve_gt)
                        for nc_ in range(4):
                            ins = t.matmul(ps_gx[:, nc_ * P:(nc_ + 1) * P],
                                           e2_bf[:, nc_ * P:(nc_ + 1) * P],
                                           gt_sb[:, :], start=True, stop=True)
                            if nc_ == 3:
                                ins.then_inc(sem["pe"], 1)
                    em("tensor", f)
                    pe_gx = bump("pe", 1)

                    def f(v, pe_gx=pe_gx, ti=ti):
                        v.wait_ge(sem["pe"], pe_gx)
                        v.tensor_copy(gex[:, :, ti * P:(ti + 1) * P],
                                      ps_gx.ap().rearrange("p (i t) -> p i t",
                                                           i=4)
                                      ).then_inc(sem["dve"], 1)
                    em("vector", f)
                    gate_dve = bump("dve")
                last_round_dve = gate_dve
                last_round_act = n["act"]
            GATES_DVE = gate_dve

            za_reader = {}
            for nc_ in range(4):
                wb = wbf[nc_ % 2]
                pc = za_pe.get(nc_ - 2, chunk_pe[len(sched) - 1]
                               if nc_ < 2 else 0)

                def f(g, nc_=nc_, pc=pc, wb=wb):
                    if pc:
                        g.wait_ge(sem["pe"], pc)
                    g.dma_start(out=wb.ap().rearrange("p (a b) -> p a b", a=DC),
                                in_=aw[nc_]).then_inc(sem["dw"], 16)
                em("gpsimd", f)
                cd = bump("dw", 16)

                for tb in range(2):
                    prev_rd = za_reader.get(tb, 0)

                    def f(t, cd=cd, tb=tb, prev_rd=prev_rd, wb=wb):
                        t.wait_ge(sem["dw"], cd)
                        if prev_rd:
                            t.wait_ge(sem["dve"], prev_rd)
                        for dc in range(DC):
                            ins = t.matmul(ps_za[tb][:, :],
                                           wb[:, dc * P:(dc + 1) * P],
                                           xnT[:, dc, tb * 512:(tb + 1) * 512],
                                           start=(dc == 0), stop=(dc == DC - 1))
                            if dc == DC - 1:
                                ins.then_inc(sem["pe"], 1)
                    em("tensor", f)
                    pe_za = bump("pe", 1)

                    def f(v, nc_=nc_, tb=tb, pe_za=pe_za):
                        v.wait_ge(sem["pe"], pe_za)
                        v.wait_ge(sem["dve"], GATES_DVE)
                        v.tensor_mul(zgT[:, nc_, tb * 512:(tb + 1) * 512],
                                     ps_za[tb][:, :],
                                     gex[:, nc_, tb * 512:(tb + 1) * 512]
                                     ).then_inc(sem["dve"], 1)
                    em("vector", f)
                    za_reader[tb] = bump("dve")
                za_pe[nc_] = n["pe"]
            if taps:
                def f(g, GATES_DVE=GATES_DVE, QR_ACT=QR_ACT, zz=n["dve"]):
                    g.wait_ge(sem["dve"], GATES_DVE)
                    g.dma_start(out=tap["gex"],
                                in_=gex.ap().rearrange("p a b -> p (a b)")
                                ).then_inc(sem["dtap"], 16)
                    g.wait_ge(sem["act"], QR_ACT)
                    g.dma_start(out=tap["qr"], in_=qr_sb[:, :]
                                ).then_inc(sem["dtap"], 16)
                    g.wait_ge(sem["dve"], zz)
                    g.dma_start(out=tap["zg"],
                                in_=zgT.ap().rearrange("p a b -> p (a b)")
                                ).then_inc(sem["dtap"], 16)
                em("gpsimd", f)
                bump("dtap", 48)

            ps_scope2.close()
            scFFN.close()

            A_DVE = n["dve"]
            A_PE = n["pe"]
            A_ACT = n["act"]

        with ExitStack() as scB1:
            EB = scB1.enter_context
            KT = [EB(nc.sbuf_tensor(f"KT{i}", [P, 64 * P], BF16))
                  for i in range(2)]
            VT = [EB(nc.sbuf_tensor(f"VT{i}", [P, 64 * P], BF16))
                  for i in range(2)]
            PT = [EB(nc.sbuf_tensor(f"PT{i}", [P, 1024], BF16)) for i in range(2)]
            pts_buf = [EB(nc.sbuf_tensor(f"pts{i}", [P, 512], BF16))
                       for i in range(2)]
            ptsum = [EB(nc.sbuf_tensor(f"ptsum{i}", [P, 512], F32))
                     for i in range(2)]
            ones_f32 = EB(nc.sbuf_tensor("ones_f32", [P, P], F32))
            lnl = [EB(nc.sbuf_tensor(f"lnl{i}", [P, 512], F32))
                   for i in range(2)]
            rec = [EB(nc.sbuf_tensor(f"rec{i}", [P, 512], F32))
                   for i in range(2)]
            masks = EB(nc.sbuf_tensor("masks_sb", [P, 8, GRP * P], BF16))

            ps_scope3 = ExitStack()
            ps_sc = [ps_scope3.enter_context(
                nc.psum_tensor(f"ps_sc{i}", [P, 2, 512], F32)) for i in range(2)]
            ps_av = [ps_scope3.enter_context(
                nc.psum_tensor(f"ps_av{i}", [P, 512], F32)) for i in range(2)]
            ps_lu = [ps_scope3.enter_context(
                nc.psum_tensor(f"ps_lu{i}", [P, 512], F32)) for i in range(2)]

            cc_k = cc_out.ap().rearrange("(c r) (d t) -> c d r t", c=NC, d=P)
            cc_v = cc_out.ap().rearrange("(c r) f -> c r f", c=NC)

            def f(sy):
                sy.wait_ge(sem["dve"], A_DVE)
                sy.wait_ge(sem["act"], A_ACT)
                sy.wait_ge(sem["pe"], A_PE)
                sy.dma_start(out=masks[:, :, :], in_=masks_in
                             ).then_inc(sem["dtab"], 16)
            em("sync", f)
            DTAB_M = bump("dtab", 16)

            def f(g):
                g.wait_ge(sem["dve"], A_DVE)
                g.wait_ge(sem["act"], A_ACT)
                g.wait_ge(sem["pe"], A_PE)
                g.memset(ones_f32[:, :], 1.0).then_inc(sem["pool"], 1)
            em("gpsimd", f)
            POOL_ONES = bump("pool")

            kv_loaded = []
            head_pe_done = [0] * NKVH

            def emit_kv_load(h):
                buf = h % 2
                pe_free = head_pe_done[h - 2] if h >= 2 else 0

                def f(sy, h=h, buf=buf, pe_free=pe_free):
                    sy.wait_ge(sem["cc"], CC_DONE)
                    if pe_free:
                        sy.wait_ge(sem["pe"], pe_free)
                    for oc in range(NC):
                        sy.dma_start(
                            out=KT[buf][:, oc * 1024:(oc + 1) * 1024]
                            .rearrange("d (r t) -> d r t", r=2),
                            in_=cc_k[oc, :, h * 2:h * 2 + 2, :]
                            ).then_inc(sem["dkv"], 16)
                        for r in range(2):
                            sy.dma_start(
                                out=VT[buf][:, (oc * 8 + r * 4) * P:
                                            (oc * 8 + r * 4 + 4) * P]
                                .rearrange("t (i d) -> t i d", i=4),
                                in_=cc_v[oc, 16 + h * 2 + r]
                                .rearrange("(t i d) -> t i d", t=P, i=4)
                                ).then_inc(sem["dkv"], 16)
                em("sync", f)
                kv_loaded.append(bump("dkv", 16 * NC * 3))

            emit_kv_load(0)
            emit_kv_load(1)

            pairs = []
            for h in range(NKVH):
                for j in range(NT):
                    ns = 8 * (j + 1)
                    for p in range(ns // 2):
                        pairs.append(dict(
                            h=h, j=j, unit=h * NT + j, p=p, npair=ns // 2,
                            s0=SLOT_OF[2 * p], s1=SLOT_OF[2 * p + 1],
                            masked=(2 * p >= ns - 8), r0=2 * p - (ns - 8)))
            NPAIR = len(pairs)
            PIPE = 2
            SKEW = 2
            exp_acts = [0] * NPAIR
            av_pes = [0] * NPAIR
            mask_dve = [0] * NPAIR
            pts_dve = [0] * NPAIR
            unit_lpe = []
            unit_norm = []
            unit_lrec = []
            first_att = True

            def emit_scores(i):
                pr = pairs[i]
                par = i % PIPE
                sc_wait = exp_acts[i - PIPE] if i >= PIPE else 0
                fa = first_att
                dkv_h = kv_loaded[pr["h"]]
                buf = pr["h"] % 2

                def f(t, pr=pr, par=par, sc_wait=sc_wait, fa=fa,
                      dkv_h=dkv_h, buf=buf):
                    if fa:
                        t.wait_ge(sem["dve"], A_DVE)
                        t.wait_ge(sem["act"], A_ACT)
                    if pr["p"] == 0:
                        t.wait_ge(sem["dkv"], dkv_h)
                    if sc_wait:
                        t.wait_ge(sem["act"], sc_wait)
                    t.matmul(ps_sc[par][:, 0, :],
                             KT[buf][:, pr["s0"] * P:(pr["s0"] + 1) * P],
                             QA[:, pr["h"], pr["j"], :, :],
                             start=True, stop=True)
                    t.matmul(ps_sc[par][:, 1, :],
                             KT[buf][:, pr["s1"] * P:(pr["s1"] + 1) * P],
                             QA[:, pr["h"], pr["j"], :, :],
                             start=True, stop=True).then_inc(sem["pe"], 1)
                em("tensor", f)
                pe_sc = bump("pe")

                pt_war_pe = av_pes[i - PIPE] if i >= PIPE else 0
                pt_war_dve = pts_dve[i - PIPE] if i >= PIPE else 0

                def f(sc, par=par, pe_sc=pe_sc, pt_war_pe=pt_war_pe,
                      pt_war_dve=pt_war_dve):
                    sc.wait_ge(sem["pe"], pe_sc)
                    if pt_war_pe:
                        sc.wait_ge(sem["pe"], pt_war_pe)
                    if pt_war_dve:
                        sc.wait_ge(sem["dve"], pt_war_dve)
                    sc.activation(PT[par][:, :],
                                  ps_sc[par].ap().rearrange("k a t -> k (a t)"),
                                  AF.Exp).then_inc(sem["act"], 1)
                em("scalar", f)
                exp_acts[i] = bump("act")

                if pr["masked"]:
                    def f(v, par=par, r0=pr["r0"], ea=exp_acts[i]):
                        v.wait_ge(sem["act"], ea)
                        v.wait_ge(sem["dtab"], DTAB_M)
                        v.tensor_mul(
                            PT[par][:, :], PT[par][:, :],
                            masks.ap().rearrange("k a t -> k (a t)")
                            [:, r0 * 512:(r0 + 2) * 512]).then_inc(sem["dve"], 1)
                    em("vector", f)
                    mask_dve[i] = bump("dve")

                def fv(v, par=par, ea=exp_acts[i], md=mask_dve[i]):
                    if not md:
                        v.wait_ge(sem["act"], ea)
                    v.tensor_add(pts_buf[par][:, :], PT[par][:, 0:512],
                                 PT[par][:, 512:1024]).then_inc(sem["dve"], 1)
                em("vector", fv)
                pts_dve[i] = bump("dve")

            def emit_av(i):
                pr = pairs[i]
                par = i % PIPE
                u = pr["unit"] % 2
                buf = pr["h"] % 2
                faw = unit_norm[pr["unit"] - 2] if pr["unit"] >= 2 else 0
                flw = unit_lrec[pr["unit"] - 2] if pr["unit"] >= 2 else 0

                def f(t, pr=pr, par=par, u=u, buf=buf, faw=faw, flw=flw,
                      ea=exp_acts[i], md=mask_dve[i], pd=pts_dve[i]):
                    if md:
                        t.wait_ge(sem["dve"], md)
                    else:
                        t.wait_ge(sem["act"], ea)
                    if pr["p"] == 0 and faw:
                        t.wait_ge(sem["dve"], faw)
                    last = pr["p"] == pr["npair"] - 1
                    t.matmul(ps_av[u][:, :],
                             VT[buf][:, pr["s0"] * P:(pr["s0"] + 1) * P],
                             PT[par][:, 0:512], start=(pr["p"] == 0),
                             stop=False)
                    t.matmul(ps_av[u][:, :],
                             VT[buf][:, pr["s1"] * P:(pr["s1"] + 1) * P],
                             PT[par][:, 512:1024], start=False, stop=last)
                    t.wait_ge(sem["dve"], pd)
                    if pr["p"] == 0 and flw:
                        t.wait_ge(sem["act"], flw)
                    t.matmul(ps_lu[u][:, :], ones_bf[:, :], pts_buf[par][:, :],
                             start=(pr["p"] == 0), stop=last
                             ).then_inc(sem["pe"], 1)
                em("tensor", f)
                av_pes[i] = bump("pe")

            def emit_tail(unit, i_last):
                u = unit % 2
                h, j = unit // NT, unit % NT
                rec_war = unit_norm[unit - 2] if unit >= 2 else 0

                c3 = n["act"]

                def f(sc, u=u, lp=av_pes[i_last], rec_war=rec_war, c3=c3):
                    sc.wait_ge(sem["pe"], lp)
                    if rec_war:
                        sc.wait_ge(sem["dve"], rec_war)
                    sc.activation(lnl[u][:, :], ps_lu[u][:, :], AF.Ln
                                  ).then_inc(sem["act"], 1)
                    sc.wait_ge(sem["act"], c3 + 1)
                    sc.activation(rec[u][:, :], lnl[u][:, :], AF.Exp,
                                  scale=-1.0).then_inc(sem["act"], 1)
                em("scalar", f)
                unit_lrec.append(bump("act", 2))

                def f(v, h=h, j=j, u=u, lr=unit_lrec[-1], ape=av_pes[i_last],
                      first=(unit == 0)):
                    v.wait_ge(sem["act"], lr)
                    v.wait_ge(sem["pe"], ape)
                    if taps and first:
                        v.wait_ge(sem["dtap"], TAP_QA1)
                    v.tensor_mul(
                        QA[:, h, j, :, :],
                        ps_av[u].ap().rearrange("d (i t) -> d i t", i=4),
                        rec[u].ap().rearrange("d (i t) -> d i t", i=4)
                        ).then_inc(sem["dve"], 1)
                em("vector", f)
                unit_norm.append(bump("dve"))

            for i in range(NPAIR + SKEW):
                if i >= SKEW:
                    emit_av(i - SKEW)
                    pr = pairs[i - SKEW]
                    if pr["p"] == pr["npair"] - 1:
                        emit_tail(pr["unit"], i - SKEW)
                        if pr["j"] == NT - 1:
                            h = pr["h"]
                            head_pe_done[h] = n["pe"]
                            if h + 2 < NKVH:
                                emit_kv_load(h + 2)
                if i < NPAIR:
                    emit_scores(i)
                    first_att = False

            if taps:
                def f(g, ad=unit_norm[-1]):
                    g.wait_ge(sem["dve"], ad)
                    g.dma_start(out=tap["qa2"],
                                in_=QA.ap().rearrange("p a b c d -> p (a b c d)")
                                ).then_inc(sem["dtap"], 16)
                em("gpsimd", f)
                bump("dtap", 16)

            ps_scope3.close()
            ATT_DVE = unit_norm[-1]
            ATT_ACT = unit_lrec[-1]
            ATT_PE = n["pe"]

        with ExitStack() as scB2:
            EB2 = scB2.enter_context
            wo_bf = [EB2(nc.sbuf_tensor(f"wo_bf{i}", [P, DC, 512], BF16))
                     for i in range(2)]
            b_bf = [EB2(nc.sbuf_tensor(f"b_bf{i}", [P, 4, 512], BF16))
                    for i in range(2)]
            x2 = [EB2(nc.sbuf_tensor(f"x2_{i}", [P, 512], F32))
                  for i in range(8)]
            out_sb = [EB2(nc.sbuf_tensor(f"out{i}", [P, 512], F32))
                      for i in range(4)]

            ps_scope4 = ExitStack()
            ps_fin = [ps_scope4.enter_context(
                nc.psum_tensor(f"ps_fin{i}", [P, 512], F32)) for i in range(4)]

            fin_pe = {}
            fin_dve = {}
            for hc in range(HC):
                pm = fin_pe.get(hc - 2, 0)

                def f(g, hc=hc, pm=pm):
                    if hc == 0:
                        g.wait_ge(sem["dve"], ATT_DVE)
                        g.wait_ge(sem["act"], ATT_ACT)
                        g.wait_ge(sem["pe"], ATT_PE)
                    if pm:
                        g.wait_ge(sem["pe"], pm)
                    g.dma_start(out=b_bf[hc % 2][:, :, :], in_=bw[hc]
                                ).then_inc(sem["dw"], 16)
                    g.dma_start(out=wo_bf[hc % 2][:, :, :], in_=wo[hc]
                                ).then_inc(sem["dw"], 16)
                em("gpsimd", f)
                dw_wo = bump("dw", 32)

                pd = fin_dve.get(hc - 1, 0)

                def f(sc, hc=hc, pd=pd):
                    if pd:
                        sc.wait_ge(sem["dve"], pd)
                    for ti in range(NT):
                        sc.dma_start(out=x2[ti][:, :],
                                     in_=x_in[ti * P:(ti + 1) * P,
                                              hc * 512:(hc + 1) * 512]
                                     ).then_inc(sem["dx2"], 16)
                em("scalar", f)
                dx2_here = bump("dx2", 16 * NT)

                for half in range(2):
                    prev_ev = fin_dve.get((hc, half - 1),
                                          fin_dve.get((hc - 1, 1), 0))

                    def f(t, hc=hc, half=half, dw_wo=dw_wo, prev_ev=prev_ev):
                        t.wait_ge(sem["dw"], dw_wo)
                        t.wait_ge(sem["dve"], ATT_DVE)
                        t.wait_ge(sem["act"], ATT_ACT)
                        if prev_ev:
                            t.wait_ge(sem["dve"], prev_ev)
                        for oc in range(NQH):
                            kvh, ql = oc // GRP, oc % GRP
                            for i in range(4):
                                ti = half * 4 + i
                                t.matmul(ps_fin[i][:, :],
                                         QA[:, kvh, ti, ql, :],
                                         wo_bf[hc % 2][:, oc, :],
                                         start=(oc == 0), stop=False)
                        for nc_ in range(4):
                            for i in range(4):
                                ti = half * 4 + i
                                ins = t.matmul(ps_fin[i][:, :],
                                               zgT[:, nc_, ti * P:(ti + 1) * P],
                                               b_bf[hc % 2][:, nc_, :],
                                               start=False, stop=(nc_ == 3))
                                if nc_ == 3 and i == 3:
                                    ins.then_inc(sem["pe"], 1)
                    em("tensor", f)
                    pe_fin = bump("pe", 1)
                    fin_pe[hc] = pe_fin

                    for i in range(4):
                        ti = half * 4 + i

                        def f(v, i=i, ti=ti, pe_fin=pe_fin, dx2_here=dx2_here):
                            v.wait_ge(sem["pe"], pe_fin)
                            v.wait_ge(sem["dx2"], dx2_here)
                            v.tensor_add(out_sb[i][:, :], ps_fin[i][:, :],
                                         x2[ti][:, :]).then_inc(sem["dve"], 1)
                        em("vector", f)
                        dve_out = bump("dve")
                        fin_dve[(hc, half)] = dve_out
                        fin_dve[hc] = dve_out

                        def f(sc, i=i, ti=ti, hc=hc, dve_out=dve_out):
                            sc.wait_ge(sem["dve"], dve_out)
                            sc.dma_start(out=y_out[ti * P:(ti + 1) * P,
                                                   hc * 512:(hc + 1) * 512],
                                         in_=out_sb[i][:, :]
                                         ).then_inc(sem["dout"], 16)
                        em("scalar", f)
                        bump("dout", 16)
            ps_scope4.close()
            DOUT_ALL = n["dout"]

            DTAP_ALL = n["dtap"]

            def f(sc):
                sc.wait_ge(sem["dout"], DOUT_ALL)
                if taps:
                    sc.wait_ge(sem["dtap"], DTAP_ALL)
            em("scalar", f)

        with nc.Block() as block:
            @block.sync
            def _(e):
                for eng, fn in plan:
                    if eng == "sync":
                        fn(e)

            @block.gpsimd
            def _(e):
                for eng, fn in plan:
                    if eng == "gpsimd":
                        fn(e)

            @block.vector
            def _(e):
                for eng, fn in plan:
                    if eng == "vector":
                        fn(e)

            @block.scalar
            def _(e):
                for eng, fn in plan:
                    if eng == "scalar":
                        fn(e)

            @block.tensor
            def _(e):
                for eng, fn in plan:
                    if eng == "tensor":
                        fn(e)

    return nc



def _tile_lhsT(w):
    O, Dd = w.shape
    a = np.ascontiguousarray(w.T).reshape(Dd // P, P, O // P, P)
    return np.ascontiguousarray(a.transpose(2, 1, 0, 3)).astype(np.float32)


def _tile_rhs(wt, ncol=512):
    K, H = wt.shape
    a = wt.reshape(K // P, P, H // ncol, ncol)
    return np.ascontiguousarray(a.transpose(2, 1, 0, 3)).astype(np.float32)


def _tables(tiles):
    pos = np.arange(S, dtype=np.float32)[:, None]
    freqs = np.exp(-np.arange(0, 128, 2, dtype=np.float32)
                   * np.float32(math.log(10000.0) / 128.0)).astype(np.float32)
    ang = (pos * freqs[None, :]).astype(np.float32)
    cos = np.cos(ang).astype(np.float32)
    sin = np.sin(ang).astype(np.float32)
    ct = np.concatenate([cos, cos], axis=1)
    st = np.concatenate([sin, -sin], axis=1)

    def per(tb, scale):
        sel = np.stack([tb[g * P:(g + 1) * P] for g in tiles], axis=0)
        sel = (sel * scale).astype(np.float32)
        return np.ascontiguousarray(
            sel.transpose(2, 0, 1).reshape(P, NT * P))
    return per(ct, SCL), per(st, SCL), per(ct, 1.0), per(st, 1.0)


def _masks(c):
    m = np.zeros((P, 8, GRP, P), dtype=np.float32)
    for r in range(8):
        if r < c:
            m[:, r] = 1.0
        elif r == c:
            tri = (np.arange(P)[None, :] >= np.arange(P)[:, None])
            m[:, r] = tri.astype(np.float32)[:, None, :]
    return np.ascontiguousarray(
        m.reshape(P, 8, GRP * P)).astype(ml_dtypes.bfloat16)


_CACHE = {}


def kernel(L0_attn_hidden_states, L0_attn_Wq, L0_attn_Wk, L0_attn_Wv,
           L0_attn_Wo, L0_ffn_A_w, L0_ffn_B_w, L0_ffn_Wq_w, L0_ffn_keys):
    x = np.ascontiguousarray(np.asarray(L0_attn_hidden_states,
                                        dtype=np.float32)[0])
    wq_t = _tile_lhsT(np.asarray(L0_attn_Wq, np.float32))
    wk_t = _tile_lhsT(np.asarray(L0_attn_Wk, np.float32))
    wv_t = _tile_lhsT(np.asarray(L0_attn_Wv, np.float32))
    aw_t = _tile_lhsT(np.asarray(L0_ffn_A_w, np.float32))
    wo_t = _tile_rhs(np.ascontiguousarray(np.asarray(L0_attn_Wo,
                                                     np.float32).T))
    bw_t = _tile_rhs(np.ascontiguousarray(np.asarray(L0_ffn_B_w,
                                                     np.float32).T))
    wqw_np = np.asarray(L0_ffn_Wq_w, np.float32)
    wqw_t = np.ascontiguousarray(
        wqw_np.T.reshape(DC, P, 16).transpose(1, 0, 2))
    keyst_np = np.ascontiguousarray(np.asarray(L0_ffn_keys, np.float32).T)
    e2_np = np.zeros((EXP, BOT), dtype=np.float32)
    for e in range(EXP):
        e2_np[e, e * 8:(e + 1) * 8] = 2.0

    if "nc" not in _CACHE:
        _CACHE["nc"] = build_nc()
    nc = _CACHE["nc"]

    in_maps = []
    for c in range(NC):
        tiles = [8 * j + c for j in range(NT)]
        xc = np.ascontiguousarray(
            np.concatenate([x[g * P:(g + 1) * P] for g in tiles], axis=0))
        tq_c, tq_s, tk_c, tk_s = _tables(tiles)
        tabs = np.ascontiguousarray(np.stack([tq_c, tq_s, tk_c, tk_s], axis=0))
        in_maps.append({
            "x_in": xc, "wq": wq_t, "wk": wk_t, "wv": wv_t, "wo": wo_t,
            "aw": aw_t, "bw": bw_t, "wqw": wqw_t, "keyst": keyst_np,
            "e2": e2_np, "tabs_in": tabs, "masks": _masks(c),
        })

    res = run_bass_kernel_spmd(nc, in_maps, core_ids=list(range(NC)))
    out = np.empty((S, D), dtype=np.float32)
    for c in range(NC):
        yc = res.results[c]["y_out"]
        for j in range(NT):
            g = 8 * j + c
            out[g * P:(g + 1) * P] = yc[j * P:(j + 1) * P]
    return out[None, :, :]
